# version 42 (final)
# baseline (speedup 1.0000x reference)
import numpy as np
from contextlib import ExitStack
import concourse.bass as bass
import concourse.mybir as mybir
from concourse.bass_utils import run_bass_kernel_spmd

F32 = mybir.dt.float32
BF16 = mybir.dt.bfloat16
AF = mybir.ActivationFunctionType
ALU = mybir.AluOpType
AX = mybir.AxisListType

D = 1024
S_TOT = 16384
NCORES = 8
C = 128
TPC = S_TOT // NCORES
NT = TPC // C
NCH = S_TOT // C
RS = D // NCORES
LR = 0.01
CCONST = 2.0 / D
EPS = 1e-5
DCH = D // 128

ENGS = ("tensor", "vector", "scalar", "sync", "gpsimd")


class Plan:
    def __init__(self):
        self.items = {e: [] for e in ENGS}
        self.ctr = {}

    def put(self, eng, fn, waits=(), inc=None, inc_n=1):
        tgt = None
        if inc is not None:
            self.ctr[inc] = self.ctr.get(inc, 0) + inc_n
            tgt = self.ctr[inc]
        ws = tuple((s, v) for (s, v) in waits if v is not None and v > 0)
        self.items[eng].append((ws, fn, inc, inc_n))
        return tgt


def realize(nc, p_, sems):
    with nc.Block() as blk:
        for engname in ENGS:
            items = p_.items[engname]
            if not items:
                continue

            def mk(items):
                def body(eng):
                    for ws, fn, inc, inc_n in items:
                        for (s, v) in ws:
                            eng.wait_ge(sems[s], v)
                        last = fn(eng)
                        if inc is not None:
                            if last is None:
                                last = eng.nop()
                            last.then_inc(sems[inc], inc_n)
                return body

            getattr(blk, engname)(mk(items))


def mk_sems(nc, es, names):
    return {n: es.enter_context(nc.semaphore(n)) for n in names}


def np_consts():
    ident = np.eye(128, dtype=np.float32)
    maskL = np.tril(np.ones((128, 128), np.float32), -1)
    maskU = np.triu(np.ones((128, 128), np.float32), 1)
    maskB = np.triu(np.ones((128, 128), np.float32), 0)
    return ident, maskL, maskU, maskB


def _wT_ap(w_d):
    return bass.AP(w_d, 0, [[D, 128], [128 * D, DCH], [1, D]])


def emit_pass_ab(nc, sems, n_tiles, T, plan=None):
    p_ = plan if plan is not None else Plan()
    with ExitStack() as _es:
        wscr = _es.enter_context(nc.sbuf_tensor("wscr", [128, 4 * D], F32))
        sw16 = T["sw16g"]
        thKh = _es.enter_context(nc.sbuf_tensor("thKh", [128, DCH * D], BF16))
        thKl = _es.enter_context(nc.sbuf_tensor("thKl", [128, DCH * D], BF16))
        thVh = _es.enter_context(nc.sbuf_tensor("thVh", [128, DCH * D], BF16))
        thVl = _es.enter_context(nc.sbuf_tensor("thVl", [128, DCH * D], BF16))
        thQh = _es.enter_context(nc.sbuf_tensor("thQh", [128, DCH * D], BF16))
        thQl = _es.enter_context(nc.sbuf_tensor("thQl", [128, DCH * D], BF16))
        xb = _es.enter_context(nc.sbuf_tensor("xb", [128, 2 * D], F32))
        x1b = _es.enter_context(nc.sbuf_tensor("x1b", [128, 2 * D], F32))
        xn2s = _es.enter_context(nc.sbuf_tensor("xn2s", [128, 2 * D], F32))
        xn = _es.enter_context(nc.sbuf_tensor("xn", [128, D], F32))
        xnTh = _es.enter_context(nc.sbuf_tensor("xnTh", [128, D], BF16))
        xnTl = _es.enter_context(nc.sbuf_tensor("xnTl", [128, D], BF16))
        scrA = _es.enter_context(nc.sbuf_tensor("scrA", [128, D], F32))
        scrS = _es.enter_context(nc.sbuf_tensor("scrS", [128, D], F32))
        yTh = _es.enter_context(nc.sbuf_tensor("yTh", [128, D], BF16))
        yTl = _es.enter_context(nc.sbuf_tensor("yTl", [128, D], BF16))
        KQt = _es.enter_context(nc.sbuf_tensor("KQt", [128, D], F32))
        VtB = _es.enter_context(nc.sbuf_tensor("VtB", [128, D], F32))
        KTb = _es.enter_context(nc.sbuf_tensor("KTb", [128, D], F32))
        QTb = _es.enter_context(nc.sbuf_tensor("QTb", [128, D], F32))
        idt = _es.enter_context(nc.sbuf_tensor("idt", [128, 128], F32))
        mkL = _es.enter_context(nc.sbuf_tensor("mkL", [128, 128], F32))
        mkU = _es.enter_context(nc.sbuf_tensor("mkU", [128, 128], F32))
        mkB = _es.enter_context(nc.sbuf_tensor("mkB", [128, 128], F32))
        Atm = _es.enter_context(nc.sbuf_tensor("Atm", [128, 128], F32))
        Btm = _es.enter_context(nc.sbuf_tensor("Btm", [128, 128], F32))
        Xb = _es.enter_context(nc.sbuf_tensor("Xb", [128, 2 * 256], F32))
        Yb = _es.enter_context(nc.sbuf_tensor("Yb", [128, 2 * 256], F32))
        Pb = _es.enter_context(nc.sbuf_tensor("Pb", [128, 2 * 128], F32))
        MhS = _es.enter_context(nc.sbuf_tensor("MhS", [128, 128], F32))
        BhS = _es.enter_context(nc.sbuf_tensor("BhS", [128, 128], F32))
        stat = _es.enter_context(nc.sbuf_tensor("stat", [128, 32], F32))
        pj0 = _es.enter_context(nc.psum_tensor("pj0", [128, 512], F32))
        pj1 = _es.enter_context(nc.psum_tensor("pj1", [128, 512], F32))
        pj2 = _es.enter_context(nc.psum_tensor("pj2", [128, 512], F32))
        pj3 = _es.enter_context(nc.psum_tensor("pj3", [128, 512], F32))
        qtr0 = _es.enter_context(nc.psum_tensor("qtr0", [128, 512], F32))
        qtr1 = _es.enter_context(nc.psum_tensor("qtr1", [128, 512], F32))
        pm_ = _es.enter_context(nc.psum_tensor("pmB", [128, 512], F32))
        pab = _es.enter_context(nc.psum_tensor("pAB", [128, 512], F32))
        pX2 = pm_[:, 128:256]
        pY2 = pm_[:, 256:384]
        pP2 = pm_[:, 384:512]
        pA = pab[:, 0:128]
        pB = pab[:, 128:256]
        qtr = [qtr0, qtr1]

        for nm, dst in (("ident", idt), ("maskL", mkL), ("maskU", mkU), ("maskB", mkB)):
            p_.put("sync", lambda s, dst=dst, nm=nm: s.dma_start(out=dst[:, :], in_=T[nm].ap()),
                   inc="bwm", inc_n=16)
        BWM = 16 * 4
        for t0_ in range(min(2, n_tiles)):
            p_.put("sync",
                   lambda s, t=t0_: s.dma_start(
                       out=xb[:, (t % 2) * D:(t % 2 + 1) * D],
                       in_=T["x"][t * C:(t + 1) * C, :]),
                   inc="al" + str(t0_ % 2), inc_n=16)
        wgates = {}

        def emit_wrounds():
            slots8 = [wscr[:, k * D:(k + 1) * D] for k in range(4)] + [
                KQt[:, :], VtB[:, :], KTb[:, :], QTb[:, :]]
            qcnt = {}
            names = ["fwT", "thVT", "thKT", "thQT"]

            def wdma(r, widx, blk):
                qeng, qsem = (("sync", "bw%d" % (r % 8)) if r % 2 == 0
                              else ("gpsimd", "bw%d" % (r % 8)))
                buf = slots8[r % 8]

                def dma(s, widx=widx, blk=blk, buf=buf):
                    return s.dma_start(out=buf, in_=bass.AP(
                        T[names[widx]], blk * 128 * D, [[D, 128], [1, D]]))
                if r < 8:
                    ws = []
                elif r < 16:
                    ws = [("bac", r - 8 + 1)]
                elif r < 24:
                    ws = [("bdv", r - 16 + 1)]
                else:
                    ws = [("bdv", r - 24 + 9)]
                qcnt[qsem] = qcnt.get(qsem, 0) + 16
                p_.put(qeng, dma, waits=ws, inc=qsem, inc_n=16)
                return (qsem, qcnt[qsem])

            dsem = [wdma(r, 0, r) for r in range(DCH)]
            for blk in range(DCH):
                ac = p_.put("scalar",
                            lambda s, off=blk * D, buf=slots8[blk]: s.sign(
                                sw16[:, off:off + D], buf),
                            waits=[dsem[blk]], inc="bac")
                assert ac == blk + 1
            wgates["AC_SW"] = DCH
            def wsplit(widx, dsth, dstl, ds, blk):
                off = blk * D
                buf = slots8[(widx * DCH + blk) % 8]
                ac = p_.put("scalar",
                            lambda s, dsth=dsth, off=off, buf=buf: s.copy(
                                dsth[:, off:off + D], buf),
                            waits=[ds], inc="bac")
                dv = p_.put(
                    "vector",
                    lambda v, dsth=dsth, dstl=dstl, off=off, buf=buf:
                    v.tensor_tensor(dstl[:, off:off + D], buf,
                                    dsth[:, off:off + D], op=ALU.subtract),
                    waits=[("bac", ac)], inc="bdv")
                assert dv == (widx - 1) * DCH + blk + 1
                return dv

            dsems = {}
            for blk in range(DCH):
                dsems[(1, blk)] = wdma(DCH + blk, 1, blk)
                wgates["DV_WV"] = wsplit(1, thVh, thVl, dsems[(1, blk)], blk)
            for widx in (2, 3):
                for blk in range(DCH):
                    dsems[(widx, blk)] = wdma(widx * DCH + blk, widx, blk)

            def emit_wrounds_kq():
                for widx, (dsth, dstl) in ((2, (thKh, thKl)), (3, (thQh, thQl))):
                    for blk in range(DCH):
                        wgates[{2: "DV_WK", 3: "DV_WQ"}[widx]] = wsplit(
                            widx, dsth, dstl, dsems[(widx, blk)], blk)
            wgates["kq"] = emit_wrounds_kq

        Z = lambda: [0] * n_tiles
        ld_x = [16 * (t // 2 + 1) for t in range(n_tiles)]
        st_x1 = [16 * (t // 2 + 1) for t in range(n_tiles)]
        dv_s1 = Z(); dv_xi1 = Z(); dv_epi = Z(); dv_s2 = Z(); dv_xi2 = Z()
        dv_lo1 = Z(); dv_ylo = Z(); dv_Vev = Z(); dv_A0 = Z(); dv_Mh = Z(); dv_Bh = Z()
        dv_Pa = [[0] * 7 for _ in range(n_tiles)]
        ac_x1 = Z(); ac_x2 = Z(); ac_m1 = Z(); ac_m2 = Z()
        ac_s1 = Z(); ac_s2 = Z()
        ac_ev1 = Z(); ac_evy = Z(); ac_Kev = Z(); ac_KTev = Z()
        ac_Qev = Z(); ac_QTev = Z(); ac_atm = Z()
        ac_lv = [[0] * 7 for _ in range(n_tiles)]
        te_tr1 = Z(); te_ffn = Z(); te_tr2 = Z(); te_V = Z(); te_K = Z()
        te_Ktr = Z(); te_Q = Z(); te_Qtr = Z(); te_A = Z(); te_B = Z()
        te_sq = [[0] * 7 for _ in range(n_tiles)]
        te_P = [[0] * 7 for _ in range(n_tiles)]
        st_v = Z(); st_kt = Z(); st_qt = Z(); st_mh = Z(); st_bh = Z()

        def slots(t):
            p = t % 2
            b1 = p * 16
            b2 = p * 16 + 8
            d = dict(
                p=p,
                xa=xb[:, p * D:(p + 1) * D],
                x1a=x1b[:, p * D:(p + 1) * D],
                xn2a=xn2s[:, p * D:(p + 1) * D])
            for nm, off in (("sx", 0), ("sq", 1), ("mu", 2), ("inv", 3),
                            ("e2", 4), ("m2", 5)):
                d[nm] = stat[:, b1 + off:b1 + off + 1]
                d[nm + "2"] = stat[:, b2 + off:b2 + off + 1]
            return d

        def sum_dve(src_ap, sx):
            def f(v):
                v.drain()
                return v.tensor_scalar(scrA[:, :], src_ap, 1.0, None,
                                       op0=ALU.mult, op1=ALU.add, accum_out=sx)
            return f

        def sq_act(src_ap, sq):
            def f(s):
                s.drain()
                return s.activation(scrS[:, :], src_ap, AF.Square, accum_out=sq)
            return f

        def mini_item(sx, sq, mu, e2, m2):
            def f(v):
                v.drain()
                v.tensor_tensor(m2, sx, sx, op=ALU.mult)
                v.tensor_scalar(e2, sq, 1.0 / D, EPS, op0=ALU.mult, op1=ALU.add)
                v.tensor_scalar(mu, sx, -1.0 / D, None, op0=ALU.mult)
                v.drain()
                return v.scalar_tensor_tensor(m2, m2, -1.0 / (D * D), e2,
                                              op0=ALU.mult, op1=ALU.add)
            return f

        def sqrt_item(m2, sq):
            def f(s):
                s.drain()
                return s.activation(sq, m2, AF.Sqrt)
            return f

        def inv_item(inv, sq, mu):
            def f(v):
                v.reciprocal(inv, sq)
                v.drain()
                return v.tensor_tensor(mu, mu, inv, op=ALU.mult)
            return f

        def trs(te, srcf):
            last = None
            for kk in range(DCH):
                last = te.transpose(qtr[kk // 4][:, (kk % 4) * 128:(kk % 4 + 1) * 128],
                                    srcf[:, kk * 128:(kk + 1) * 128], idt[:, :])
            return last

        def ev_hi(s, dst):
            s.copy(dst[:, 0:512], qtr0[:, :])
            return s.copy(dst[:, 512:1024], qtr1[:, :])

        def ev_lo(v, dsth, dstl):
            v.tensor_tensor(dstl[:, 0:512], qtr0[:, :], dsth[:, 0:512],
                            op=ALU.subtract)
            return v.tensor_tensor(dstl[:, 512:1024], qtr1[:, :],
                                   dsth[:, 512:1024], op=ALU.subtract)

        def ev_f32(s, dst, pja, pjb):
            s.copy(dst[:, 0:512], pja[:, :])
            return s.copy(dst[:, 512:1024], pjb[:, :])

        def proj_hi(te, xh, wh, pja, pjb, stop=False):
            last = None
            for h, pjx in ((0, pja), (1, pjb)):
                for kk in range(DCH):
                    last = te.matmul(
                        pjx[:, :], xh[:, kk * 128:(kk + 1) * 128],
                        wh[:, kk * D + h * 512: kk * D + h * 512 + 512],
                        start=(kk == 0), stop=(stop and kk == DCH - 1))
            return last

        def proj_rest(te, xh, xl, wh, wl, pja, pjb):
            last = None
            for h, pjx in ((0, pja), (1, pjb)):
                for pi, (xp, wp) in enumerate(((xl, wh), (xh, wl))):
                    for kk in range(DCH):
                        last = te.matmul(
                            pjx[:, :], xp[:, kk * 128:(kk + 1) * 128],
                            wp[:, kk * D + h * 512: kk * D + h * 512 + 512],
                            start=False, stop=(pi == 1 and kk == DCH - 1))
            return last

        def chain_sq(u, k):
            if u < 0 or u >= n_tiles:
                return
            q = u % 2
            xo = Xb[:, q * 256 + ((k - 1) % 2) * 128: q * 256 + ((k - 1) % 2) * 128 + 128]
            yo = Yb[:, q * 256 + ((k - 1) % 2) * 128: q * 256 + ((k - 1) % 2) * 128 + 128]
            xn_ = Xb[:, q * 256 + (k % 2) * 128: q * 256 + (k % 2) * 128 + 128]
            yn_ = Yb[:, q * 256 + (k % 2) * 128: q * 256 + (k % 2) * 128 + 128]

            if k == 1:
                w_dv = dv_Pa[u - 1][6] if u >= 1 else dv_A0[u]
                w_ac = ac_lv[u - 1][6] if u >= 1 else None
            else:
                w_dv = dv_Pa[u][k - 1]
                w_ac = ac_lv[u][k - 1]

            def sq(te, xo=xo, yo=yo):
                te.matmul(pX2, yo, xo, start=True, stop=True)
                return te.matmul(pY2, xo, yo, start=True, stop=True)
            te_sq[u][k] = p_.put("tensor", sq,
                                 waits=[("bdv", w_dv), ("bac", w_ac)], inc="bte")

            def lev(s, xn_=xn_, yn_=yn_):
                s.copy(xn_, pX2)
                return s.copy(yn_, pY2)
            ac_lv[u][k] = p_.put("scalar", lev, waits=[("bte", te_sq[u][k])], inc="bac")

        def chain_p(u, k):
            if u < 0 or u >= n_tiles:
                return
            q = u % 2
            xn_ = Xb[:, q * 256 + (k % 2) * 128: q * 256 + (k % 2) * 128 + 128]
            Pq = Pb[:, q * 128:(q + 1) * 128]
            w_dv = (dv_Pa[u - 1][6] if u >= 1 else dv_A0[u]) if k == 1 \
                else dv_Pa[u][k - 1]
            te_P[u][k] = p_.put(
                "tensor",
                lambda te, xn_=xn_, Pq=Pq: te.matmul(pP2, xn_, Pq, start=True, stop=True),
                waits=[("bac", ac_lv[u][k]), ("bdv", w_dv)], inc="bte")
            dv_Pa[u][k] = p_.put(
                "vector",
                lambda v, Pq=Pq: v.tensor_tensor(Pq, Pq, pP2, op=ALU.add),
                waits=[("bte", te_P[u][k])], inc="bdv")

        def chain_mh(u):
            if u < 0 or u >= n_tiles:
                return
            q = u % 2
            Pq = Pb[:, q * 128:(q + 1) * 128]

            def mhf(v, Pq=Pq):
                v.drain()
                return v.tensor_scalar_mul(MhS[:, :], Pq, -(LR * CCONST))
            dv_Mh[u] = p_.put("vector", mhf,
                              waits=[("bdv", dv_Pa[u][6]),
                                     ("bsm", st_mh[u - 1] if u >= 1 else None)],
                              inc="bdv")
            st_mh[u] = p_.put(
                "gpsimd",
                lambda g, u=u: g.dma_start(out=T["mh_out"](u), in_=MhS[:, :]),
                waits=[("bdv", dv_Mh[u])], inc="bsm", inc_n=16)
            hk = T.get("tile_done_hook")
            if hk:
                hk(p_, u, dict(st_v=st_v, st_kt=st_kt, st_qt=st_qt,
                               st_mh=st_mh, st_bh=st_bh))

        def fA(t):
            if t >= n_tiles:
                return
            S_ = slots(t)
            AL = "al" + str(S_["p"])
            if t >= 2:
                assert ld_x[t] == p_.put(
                    "sync",
                    lambda s, xa=S_["xa"], t=t: s.dma_start(
                        out=xa, in_=T["x"][t * C:(t + 1) * C, :]),
                    waits=[("adv", dv_epi[t - 2])],
                    inc=AL, inc_n=16)
            dv_s1[t] = p_.put("vector", sum_dve(S_["xa"], S_["sx"]),
                              waits=[(AL, ld_x[t]),
                                     ("aac", ac_m1[t - 2] if t >= 2 else None)],
                              inc="adv")

            def sq1f(v, S_=S_):
                v.drain()
                return v.scalar_tensor_tensor(
                    scrA[:, :], S_["xa"], 1.0, S_["xa"],
                    op0=ALU.mult, op1=ALU.mult, accum_out=S_["sq"])
            ac_s1[t] = 0
            p_.put("vector", sq1f,
                   waits=[("adv", dv_xi1[t - 2] if t >= 2 else None)],
                   inc="adv")

        def fB(t):
            if t >= n_tiles:
                return
            S_ = slots(t)
            dv_m = p_.put("vector",
                          mini_item(S_["sx"], S_["sq"], S_["mu"], S_["e2"], S_["m2"]),
                          waits=[("adv", dv_s1[t])], inc="adv")
            ac_m1[t] = p_.put("scalar", sqrt_item(S_["m2"], S_["sq"]),
                              waits=[("adv", dv_m)], inc="aac")
            dv_xi1[t] = p_.put("vector", inv_item(S_["inv"], S_["sq"], S_["mu"]),
                               waits=[("aac", ac_m1[t])], inc="adv")

        def fC(t):
            if t >= n_tiles:
                return
            S_ = slots(t)
            ac_x1[t] = p_.put(
                "scalar",
                lambda s, S_=S_: s.activation(xn[:, :], S_["xa"], AF.Identity,
                                              scale=S_["inv"], bias=S_["mu"]),
                waits=[("adv", dv_xi1[t]),
                       ("ate", te_tr1[t - 1] if t >= 1 else None)],
                inc="aac")

        fA(0); fB(0); fC(0)
        fA(1); fB(1)
        emit_wrounds()
        AC_SW = wgates["AC_SW"]
        DV_WV = wgates["DV_WV"]
        for t in range(n_tiles + 1):
            u = t - 1
            for fn_, ws_, inc_, incn_ in T.get("ac_at", {}).get(t, []):
                p_.put("scalar", fn_, waits=ws_, inc=inc_, inc_n=incn_)
            if t < n_tiles:
                S_ = slots(t)
                p = S_["p"]

                te_tr1[t] = p_.put("tensor", lambda te: trs(te, xn),
                                   waits=[("aac", ac_x1[t]), ("bwm", BWM),
                                          ("bac", ac_QTev[t - 1] if t >= 1 else None)],
                                   inc="ate")
                ac_ev1[t] = p_.put("scalar", lambda s: ev_hi(s, xnTh),
                                   waits=[("ate", te_tr1[t])], inc="aac")
                dv_lo1[t] = p_.put("vector", lambda v: ev_lo(v, xnTh, xnTl),
                                   waits=[("aac", ac_ev1[t]), ("ate", te_tr1[t])],
                                   inc="adv")

                def ffn_p(te, xp, pjx, h, start, stop):
                    last = None
                    for kk in range(DCH):
                        last = te.matmul(
                            pjx[:, :], xp[:, kk * 128:(kk + 1) * 128],
                            sw16[:, kk * D + h * 512: kk * D + h * 512 + 512],
                            start=(start and kk == 0),
                            stop=(stop and kk == DCH - 1))
                    return last
                p_.put("tensor",
                       lambda te: (ffn_p(te, xnTh, pj0, 0, True, False),
                                   ffn_p(te, xnTh, pj1, 1, True, False))[-1],
                       waits=[("aac", ac_ev1[t]),
                              ("bac", AC_SW if t == 0 else None),
                              ("adv", dv_epi[t - 1] if t >= 1 else None),
                              ("bac", ac_Qev[t - 1] if t >= 1 else None)],
                       inc="bte")
                te_f0 = p_.put("tensor",
                               lambda te: ffn_p(te, xnTl, pj0, 0, False, True),
                               waits=[("adv", dv_lo1[t])], inc="bte")
                te_ffn[t] = p_.put("tensor",
                                   lambda te: ffn_p(te, xnTl, pj1, 1, False, True),
                                   inc="bte")

                ha = stat[:, p * 16 + 6:p * 16 + 7]
                hb = stat[:, p * 16 + 7:p * 16 + 8]

                def epi0(v, S_=S_, ha=ha):
                    v.drain()
                    return v.scalar_tensor_tensor(
                        S_["x1a"][:, 0:512], pj0[:, :], 0.5, S_["xa"][:, 0:512],
                        op0=ALU.mult, op1=ALU.add, accum_out=ha)
                dv_e0 = p_.put(
                    "vector", epi0,
                    waits=[("bte", te_f0),
                           ("ax" + str(p), st_x1[t - 2] if t >= 2 else None)],
                    inc="adv")

                def epi1(v, S_=S_, ha=ha, hb=hb):
                    v.scalar_tensor_tensor(S_["x1a"][:, 512:1024], pj1[:, :],
                                           0.5, S_["xa"][:, 512:1024],
                                           op0=ALU.mult, op1=ALU.add, accum_out=hb)
                    v.drain()
                    return v.tensor_tensor(S_["sx2"], ha, hb, op=ALU.add)
                dv_epi[t] = p_.put(
                    "vector", epi1,
                    waits=[("bte", te_ffn[t])],
                    inc="adv")
                assert st_x1[t] == p_.put(
                    "gpsimd",
                    lambda g, S_=S_, t=t: g.dma_start(
                        out=T["x1"][t * C:(t + 1) * C, :], in_=S_["x1a"]),
                    waits=[("adv", dv_epi[t])], inc="ax" + str(p), inc_n=16)

                for fn_, ws_, inc_, incn_ in T.get("gps_at", {}).get(t, []):
                    p_.put("gpsimd", fn_, waits=ws_, inc=inc_, inc_n=incn_)

                fC(t + 1)

                def sq2f(v, S_=S_):
                    v.drain()
                    return v.scalar_tensor_tensor(
                        scrA[:, :], S_["x1a"], 1.0, S_["x1a"],
                        op0=ALU.mult, op1=ALU.mult, accum_out=S_["sq2"])
                p_.put("vector", sq2f,
                       waits=[("adv", dv_xi2[t - 2] if t >= 2 else None)],
                       inc="adv")

                def mini2(v, S_=S_):
                    v.drain()
                    v.tensor_tensor(S_["m22"], S_["sx2"], S_["sx2"], op=ALU.mult)
                    v.tensor_scalar(S_["e22"], S_["sq2"], 1.0 / D, EPS,
                                    op0=ALU.mult, op1=ALU.add)
                    v.tensor_scalar(S_["mu2"], S_["sx2"], 1.0 / D, None,
                                    op0=ALU.mult)
                    v.drain()
                    return v.scalar_tensor_tensor(S_["m22"], S_["m22"],
                                                  -1.0 / (D * D), S_["e22"],
                                                  op0=ALU.mult, op1=ALU.add)
                dv_m2 = p_.put("vector", mini2, waits=[], inc="adv")
                ac_m2[t] = p_.put("scalar", sqrt_item(S_["m22"], S_["sq2"]),
                                  waits=[("adv", dv_m2)], inc="aac")

                def invapp2(v, S_=S_):
                    v.reciprocal(S_["inv2"], S_["sq2"])
                    v.drain()
                    return v.tensor_scalar(S_["xn2a"], S_["x1a"], S_["mu2"],
                                           S_["inv2"], op0=ALU.subtract,
                                           op1=ALU.mult)
                dv_xi2[t] = p_.put("vector", invapp2,
                                   waits=[("aac", ac_m2[t])], inc="adv")

                te_tr2[t] = p_.put("tensor", lambda te, S_=S_: trs(te, S_["xn2a"]),
                                   waits=[("adv", dv_xi2[t]), ("aac", ac_ev1[t]),
                                          ("adv", dv_lo1[t])],
                                   inc="ate")
                ac_evy[t] = p_.put("scalar", lambda s: ev_hi(s, yTh),
                                   waits=[("ate", te_tr2[t])], inc="aac")
                dv_ylo[t] = p_.put("vector", lambda v: ev_lo(v, yTh, yTl),
                                   waits=[("aac", ac_evy[t]), ("ate", te_tr2[t])],
                                   inc="adv")

                if t == 0:
                    wgates["kq"]()
                    DV_WK = wgates["DV_WK"]
                    DV_WQ = wgates["DV_WQ"]

                chain_sq(u, 1)

                p_.put("tensor", lambda te: proj_hi(te, yTh, thVh, pj0, pj1),
                       waits=[("aac", ac_evy[t]), ("adv", dv_epi[t]),
                              ("bdv", DV_WV if t == 0 else None)],
                       inc="bte")
                te_V[t] = p_.put("tensor",
                                 lambda te: proj_rest(te, yTh, yTl, thVh, thVl,
                                                      pj0, pj1),
                                 waits=[("adv", dv_ylo[t])], inc="bte")
                dv_Vev[t] = p_.put(
                    "scalar", lambda s: ev_f32(s, VtB, pj0, pj1),
                    waits=[("bte", te_V[t]),
                           ("bdv", 22 if t == 0 else None),
                           ("bsv", st_v[t - 1] if t >= 1 else None)],
                    inc="bac")
                st_v[t] = p_.put(
                    "gpsimd",
                    lambda g, t=t: g.dma_start(out=T["v_out"](t), in_=VtB[:, :]),
                    waits=[("bac", dv_Vev[t])], inc="bsv", inc_n=16)

                chain_p(u, 1)

                p_.put("tensor", lambda te: proj_hi(te, yTh, thKh, pj2, pj3),
                       waits=[("bdv", DV_WK if t == 0 else None)], inc="bte")
                te_K[t] = p_.put("tensor",
                                 lambda te: proj_rest(te, yTh, yTl, thKh, thKl,
                                                      pj2, pj3),
                                 inc="bte")
                ac_Kev[t] = p_.put("scalar", lambda s: ev_f32(s, KQt, pj2, pj3),
                                   waits=[("bte", te_K[t]),
                                          ("bdv", 21 if t == 0 else None)],
                                   inc="bac")
                chain_sq(u, 2)
                te_Ktr[t] = p_.put("tensor", lambda te: trs(te, KQt),
                                   waits=[("bac", ac_Kev[t]),
                                          ("adv", dv_ylo[t])], inc="bte")
                ac_KTev[t] = p_.put("scalar", lambda s: ev_hi(s, KTb),
                                    waits=[("bte", te_Ktr[t]),
                                           ("bdv", 23 if t == 0 else None),
                                           ("bsk", st_kt[t - 1] if t >= 1 else None)],
                                    inc="bac")
                st_kt[t] = p_.put(
                    "gpsimd",
                    lambda g, t=t: g.dma_start(out=T["kt_out"](t), in_=KTb[:, :]),
                    waits=[("bac", ac_KTev[t])], inc="bsk", inc_n=16)
                chain_p(u, 2)

                p_.put("tensor", lambda te: proj_hi(te, yTh, thQh, pj0, pj1),
                       waits=[("bac", dv_Vev[t]),
                              ("bdv", DV_WQ if t == 0 else None)], inc="bte")
                te_Q[t] = p_.put("tensor",
                                 lambda te: proj_rest(te, yTh, yTl, thQh, thQl,
                                                      pj0, pj1),
                                 inc="bte")
                ac_Qev[t] = p_.put("scalar", lambda s: ev_f32(s, KQt, pj0, pj1),
                                   waits=[("bte", te_Q[t]), ("bte", te_Ktr[t])],
                                   inc="bac")
                chain_sq(u, 3)
                te_Qtr[t] = p_.put("tensor", lambda te: trs(te, KQt),
                                   waits=[("bac", ac_Qev[t])], inc="bte")
                ac_QTev[t] = p_.put("scalar", lambda s: ev_hi(s, QTb),
                                    waits=[("bte", te_Qtr[t]),
                                           ("bdv", 24 if t == 0 else None),
                                           ("bsq", st_qt[t - 1] if t >= 1 else None)],
                                    inc="bac")
                st_qt[t] = p_.put(
                    "gpsimd",
                    lambda g, t=t: g.dma_start(out=T["qt_out"](t), in_=QTb[:, :]),
                    waits=[("bac", ac_QTev[t])], inc="bsq", inc_n=16)
                chain_p(u, 3)

                def amm(te):
                    last = None
                    for kk in range(DCH):
                        last = te.matmul(pA, KTb[:, kk * 128:(kk + 1) * 128],
                                         KTb[:, kk * 128:(kk + 1) * 128],
                                         start=(kk == 0), stop=(kk == DCH - 1))
                    return last
                te_A[t] = p_.put("tensor", amm,
                                 waits=[("bac", ac_KTev[t]),
                                        ("bdv", dv_Bh[t - 1] if t >= 1 else None)],
                                 inc="bte")
                ac_atm[t] = p_.put(
                    "scalar",
                    lambda s: s.activation(Atm[:, :], pA, AF.Copy,
                                           scale=-(CCONST * LR),
                                           bias=-(CCONST * LR)),
                    waits=[("bte", te_A[t])], inc="bac")

                def a0(v, q=p):
                    v.tensor_tensor(Xb[:, q * 256:q * 256 + 128], Atm[:, :],
                                    mkL[:, :], op=ALU.mult)
                    v.tensor_tensor(Yb[:, q * 256:q * 256 + 128], Atm[:, :],
                                    mkU[:, :], op=ALU.mult)
                    v.drain()
                    return v.tensor_tensor(Pb[:, q * 128:(q + 1) * 128],
                                           Yb[:, q * 256:q * 256 + 128],
                                           idt[:, :], op=ALU.add)
                dv_A0[t] = p_.put("vector", a0, waits=[("bac", ac_atm[t])], inc="bdv")

                chain_sq(u, 4)
                chain_p(u, 4)

                def bmm(te):
                    last = None
                    for kk in range(DCH):
                        last = te.matmul(pB, KTb[:, kk * 128:(kk + 1) * 128],
                                         QTb[:, kk * 128:(kk + 1) * 128],
                                         start=(kk == 0), stop=(kk == DCH - 1))
                    return last
                te_B[t] = p_.put("tensor", bmm,
                                 waits=[("bac", ac_atm[t])], inc="bte")

                def bhf(v):
                    v.tensor_scalar_add(Btm[:, :], pB, 1.0)
                    v.drain()
                    return v.tensor_tensor(BhS[:, :], Btm[:, :], mkB[:, :],
                                           op=ALU.mult)
                dv_Bh[t] = p_.put("vector", bhf,
                                  waits=[("bte", te_B[t]),
                                         ("bsb", st_bh[t - 1] if t >= 1 else None)],
                                  inc="bdv")
                st_bh[t] = p_.put(
                    "gpsimd",
                    lambda g, t=t: g.dma_start(out=T["bh_out"](t), in_=BhS[:, :]),
                    waits=[("bdv", dv_Bh[t])], inc="bsb", inc_n=16)

                chain_sq(u, 5)
                chain_p(u, 5)
                chain_sq(u, 6)
                chain_p(u, 6)
                chain_mh(u)

                fA(t + 2)
                fB(t + 2)
            else:
                for k in range(1, 7):
                    chain_sq(u, k)
                    chain_p(u, k)
                chain_mh(u)

        if plan is None:
            realize(nc, p_, sems)
    L = n_tiles - 1
    return dict(ax0=p_.ctr.get("ax0", 0), ax1=p_.ctr.get("ax1", 0),
                bte=p_.ctr.get("bte", 0), adv=p_.ctr.get("adv", 0),
                aac=p_.ctr.get("aac", 0), bdv=p_.ctr.get("bdv", 0),
                bac=p_.ctr.get("bac", 0),
                epi=dv_epi[L], ylo=dv_ylo[L], evy=ac_evy[L],
                kev=ac_Kev[L], atm=ac_atm[L], lv=ac_lv[L][6],
                pa=dv_Pa[L][6], bh=dv_Bh[L], tek=te_K[L],
                qev=ac_Qev[L], vev=dv_Vev[L], vev_sem="bac")


A_SEMS = ("al0", "al1", "ax0", "ax1", "ate", "adv", "aac")
B_SEMS = ("bw0", "bw1", "bw2", "bw3", "bw4", "bw5", "bw6", "bw7", "bwq",
          "bwm", "bsv", "bsk", "bsq", "bsm", "bsb", "bte", "bdv", "bac")


def emit_phase2(nc, sems, n_chunks, T, plan=None):
    p_ = plan if plan is not None else Plan()
    with ExitStack() as _es:
        KTc = _es.enter_context(nc.sbuf_tensor("KTc", [128, 4 * D], F32))
        QTc = _es.enter_context(nc.sbuf_tensor("QTc", [128, 4 * D], F32))
        Ktm = _es.enter_context(nc.sbuf_tensor("Ktm", [128, D], F32))
        Vc = _es.enter_context(nc.sbuf_tensor("Vc", [128, 4 * 128], F32))
        Mh2 = _es.enter_context(nc.sbuf_tensor("Mh2", [128, 4 * 128], F32))
        Bh2 = _es.enter_context(nc.sbuf_tensor("Bh2", [128, 4 * 128], F32))
        St = _es.enter_context(nc.sbuf_tensor("St", [128, DCH * 128], F32))
        btS = _es.enter_context(nc.sbuf_tensor("btS", [1, 128], F32))
        Usb = _es.enter_context(nc.sbuf_tensor("Usb", [128, 128], F32))
        Gsb = _es.enter_context(nc.sbuf_tensor("Gsb", [128, 128], F32))
        Osb = _es.enter_context(nc.sbuf_tensor("Osb", [128, 8 * 128], F32))
        one1C = _es.enter_context(nc.sbuf_tensor("one1C", [1, 128], F32))
        oneC1 = _es.enter_context(nc.sbuf_tensor("oneC1", [128, 1], F32))
        idt3 = _es.enter_context(nc.sbuf_tensor("idt3", [128, 128], F32))
        ptrA = _es.enter_context(nc.psum_tensor("ptrA", [128, 512], F32))
        ptrB = _es.enter_context(nc.psum_tensor("ptrB", [128, 512], F32))
        pU = _es.enter_context(nc.psum_tensor("pU", [128, 128], F32))
        pG = _es.enter_context(nc.psum_tensor("pG", [128, 128], F32))
        pO = _es.enter_context(nc.psum_tensor("pO", [128, 128], F32))
        pbp = _es.enter_context(nc.psum_tensor("pbp", [1, 128], F32))
        pupA = _es.enter_context(nc.psum_tensor("pupA", [128, 512], F32))
        pupB = _es.enter_context(nc.psum_tensor("pupB", [128, 512], F32))
        ptr = [ptrA, ptrB]
        pup = [pupA, pupB]

        entry_w = T.get("entry_waits", [])
        p_.put("sync", lambda s: s.dma_start(
            out=St[:, :], in_=bass.AP(T["w0t"], 0, [[128, 128], [128 * 128, DCH], [1, 128]])),
            waits=entry_w, inc="cli", inc_n=16)
        p_.put("sync", lambda s: s.dma_start(out=btS[:, :], in_=T["b0s"].ap()),
               inc="cli", inc_n=16)
        p_.put("sync", lambda s: s.dma_start(out=idt3[:, :], in_=T["ident"].ap()),
               inc="cli", inc_n=16)
        dv_ones = p_.put(
            "vector",
            lambda v: (v.drain(), v.memset(one1C[:, :], 1.0),
                       v.memset(oneC1[:, :], 1.0))[-1],
            waits=entry_w, inc="cdv")
        CLI = 16 * 3
        for fn_, ws_, inc_, incn_ in T.get("gps_pre", []):
            p_.put("gpsimd", fn_, waits=ws_, inc=inc_, inc_n=incn_)
        for fn_, ws_, inc_, incn_ in T.get("sync_pre", []):
            p_.put("sync", fn_, waits=ws_, inc=inc_, inc_n=incn_)
        for fn_, ws_, inc_, incn_ in T.get("ac_pre", []):
            p_.put("scalar", fn_, waits=ws_, inc=inc_, inc_n=incn_)

        ld_c = [80 * (j // 4 + 1) for j in range(n_chunks)]
        te_tr = [0] * n_chunks
        te_U = [0] * n_chunks; te_G = [0] * n_chunks
        te_O = [0] * n_chunks; te_upd = [0] * n_chunks
        for j in range(n_chunks):
            te_tr[j] = 1 if j == 0 else 5 * j + 1
            te_U[j] = 5 * j + 2
            te_G[j] = 5 * j + 3
            te_O[j] = 5 * j + 4
            te_upd[j] = 5 * j + 5
        dv_Uev = [3 * j + 2 for j in range(n_chunks)]
        dv_Oev = [3 * j + 3 for j in range(n_chunks)]
        dv_St = [3 * j + 4 for j in range(n_chunks)]
        ac_Ktm = [2 * j + 1 for j in range(n_chunks)]
        ac_G = [2 * j + 2 for j in range(n_chunks)]
        st_O = [16 * (j // 8 + 1) for j in range(n_chunks)]

        extra_w = T.get("ld_waits", lambda j: [])

        for j in range(n_chunks):
            p = j % 4
            CL = "cl" + str(p)
            SO = "so%d" % (j % 8)
            KTa = KTc[:, p * D:(p + 1) * D]
            QTa = QTc[:, p * D:(p + 1) * D]
            Va = Vc[:, p * 128:(p + 1) * 128]
            Mha = Mh2[:, p * 128:(p + 1) * 128]
            Bha = Bh2[:, p * 128:(p + 1) * 128]
            Oa = Osb[:, (j % 8) * 128:(j % 8 + 1) * 128]

            for fn_, ws_, inc_, incn_ in T.get("sync_pre_at", {}).get(j, []):
                p_.put("sync", fn_, waits=ws_, inc=inc_, inc_n=incn_)
            for fn_, ws_, inc_, incn_ in T.get("gps_pre_at", {}).get(j, []):
                p_.put("gpsimd", fn_, waits=ws_, inc=inc_, inc_n=incn_)

            def lds(s, KTa=KTa, QTa=QTa, Va=Va, Mha=Mha, Bha=Bha, j=j, CL=CL):
                s.dma_start(out=KTa, in_=T["kt_in"](j)).then_inc(sems[CL], 16)
                s.dma_start(out=QTa, in_=T["qt_in"](j)).then_inc(sems[CL], 16)
                s.dma_start(out=Va, in_=T["v_in"](j)).then_inc(sems[CL], 16)
                s.dma_start(out=Mha, in_=T["mh_in"](j)).then_inc(sems[CL], 16)
                return s.dma_start(out=Bha, in_=T["bh_in"](j))
            p_.ctr[CL] = p_.ctr.get(CL, 0) + 64
            assert ld_c[j] == p_.put(
                "sync", lds,
                waits=[("cte", te_upd[j - 4] if j >= 4 else None),
                       ("cdv", dv_Uev[j - 4] if j >= 4 else None)] + extra_w(j),
                inc=CL, inc_n=16)

            def trs(te, KTa=KTa):
                last = None
                for kk in range(DCH):
                    last = te.transpose(ptr[kk // 4][:, (kk % 4) * 128:(kk % 4 + 1) * 128],
                                        KTa[:, kk * 128:(kk + 1) * 128], idt3[:, :])
                return last

            if j == 0:
                assert te_tr[0] == p_.put(
                    "tensor", trs,
                    waits=[("cl0", ld_c[0]), ("cli", CLI)]
                    + T.get("w_trs0", []),
                    inc="cte")

            def umm(te, KTa=KTa):
                for kk in range(DCH):
                    te.matmul(pU[:, :], KTa[:, kk * 128:(kk + 1) * 128],
                              St[:, kk * 128:(kk + 1) * 128],
                              start=(kk == 0), stop=False)
                return te.matmul(pU[:, :], one1C[:, :], btS[:, :],
                                 start=False, stop=True)
            assert te_U[j] == p_.put(
                "tensor", umm,
                waits=[("cdv", dv_St[j - 1] if j >= 1 else dv_ones),
                       (CL, ld_c[j])]
                + (T.get("w_u0", []) if j == 0 else []),
                inc="cte")

            def qmm(te, QTa=QTa, half=0):
                last = None
                for kk in range(half * 4, half * 4 + 4):
                    last = te.matmul(pO[:, :], QTa[:, kk * 128:(kk + 1) * 128],
                                     St[:, kk * 128:(kk + 1) * 128],
                                     start=(kk == 0), stop=False)
                if half == 0:
                    return last
                return te.matmul(pO[:, :], one1C[:, :], btS[:, :],
                                 start=False, stop=False)
            p_.put("tensor", lambda te, QTa=QTa: qmm(te, QTa, 0),
                   waits=(T.get("w_qs0", []) if j == 0 else []))

            assert te_G[j] == p_.put(
                "tensor",
                lambda te, Mha=Mha: te.matmul(pG[:, :], Mha, Usb[:, :],
                                              start=True, stop=True),
                waits=[("cdv", dv_Uev[j]), ("cac", ac_Ktm[j])]
                + (T.get("w_g0", []) if j == 0 else []),
                inc="cte")

            p_.put("tensor", lambda te, QTa=QTa: qmm(te, QTa, 1))

            assert te_O[j] == p_.put(
                "tensor",
                lambda te, Bha=Bha: te.matmul(pO[:, :], Bha, Gsb[:, :],
                                              start=False, stop=True),
                waits=[("cac", ac_G[j])],
                inc="cte")

            def updmm(te):
                for kk in range(DCH):
                    te.matmul(pup[kk // 4][:, (kk % 4) * 128:(kk % 4 + 1) * 128],
                              Ktm[:, kk * 128:(kk + 1) * 128], Gsb[:, :],
                              start=True, stop=True)
                return te.matmul(pbp[:, :], oneC1[:, :], Gsb[:, :],
                                 start=True, stop=True)
            assert te_upd[j] == p_.put(
                "tensor", updmm,
                waits=[("cac", ac_G[j])]
                + (T.get("w_upd0", []) if j == 0 else []),
                inc="cte")

            if j + 1 < n_chunks:
                pn = (j + 1) % 4
                KTn = KTc[:, pn * D:(pn + 1) * D]
                assert te_tr[j + 1] == p_.put(
                    "tensor", lambda te, KTn=KTn: trs(te, KTn),
                    waits=[("cl" + str(pn), ld_c[j + 1]),
                           ("cac", ac_Ktm[j])],
                    inc="cte")

            assert dv_Uev[j] == p_.put(
                "vector",
                lambda v, Va=Va: v.tensor_tensor(Usb[:, :], pU[:, :], Va, op=ALU.subtract),
                waits=[("cte", te_U[j]), (CL, ld_c[j])],
                inc="cdv")
            assert dv_Oev[j] == p_.put(
                "vector",
                lambda v, Oa=Oa: v.tensor_copy(Oa, pO[:, :]),
                waits=[("cte", te_O[j]),
                       (SO, st_O[j - 8] if j >= 8 else None)],
                inc="cdv")

            def stupd(v):
                v.tensor_tensor(St[:, 0:512], St[:, 0:512], pupA[:, :], op=ALU.add)
                v.tensor_tensor(St[:, 512:1024], St[:, 512:1024], pupB[:, :], op=ALU.add)
                return v.tensor_tensor(btS[:, :], btS[:, :], pbp[:, :], op=ALU.add)
            assert dv_St[j] == p_.put("vector", stupd, waits=[("cte", te_upd[j])],
                                      inc="cdv")

            def ktmev(s):
                s.copy(Ktm[:, 0:512], ptrA[:, :])
                return s.copy(Ktm[:, 512:1024], ptrB[:, :])
            assert ac_Ktm[j] == p_.put("scalar", ktmev,
                                       waits=[("cte", te_tr[j])], inc="cac")
            assert ac_G[j] == p_.put("scalar", lambda s: s.copy(Gsb[:, :], pG[:, :]),
                                     waits=[("cte", te_G[j])], inc="cac")

            assert st_O[j] == p_.put(
                "gpsimd",
                lambda g, Oa=Oa, j=j: g.dma_start(out=T["o_out"](j), in_=Oa),
                waits=[("cdv", dv_Oev[j])], inc=SO, inc_n=16)
            hk = T.get("chunk_done_hook")
            if hk:
                hk(p_, j, st_O)

        if plan is None:
            realize(nc, p_, sems)
    return dict(cte=p_.ctr.get("cte", 0), cdv=p_.ctr.get("cdv", 0),
                cac=p_.ctr.get("cac", 0),
                so=[("so%d" % k, p_.ctr.get("so%d" % k, 0)) for k in range(8)])


C_SEMS = tuple(["cl0", "cl1", "cl2", "cl3", "cli", "cte", "cdv", "cac"]
               + ["so%d" % k for k in range(8)])


def emit_phase3(nc, sems, n_tiles, T, plan=None):
    p_ = plan if plan is not None else Plan()
    nh = 2 if n_tiles >= 2 else 1
    H = n_tiles // nh
    with ExitStack() as _es:
        sw16 = T["sw16g"]
        x1t = _es.enter_context(nc.sbuf_tensor("x1t", [128, 4 * D], F32))
        ot = _es.enter_context(nc.sbuf_tensor("ot", [128, 8 * D], F32))
        x2b = _es.enter_context(nc.sbuf_tensor("x2b", [128, 4 * D], F32))
        x3b = _es.enter_context(nc.sbuf_tensor("x3b", [128, 2 * D], F32))
        xn3 = _es.enter_context(nc.sbuf_tensor("xn3", [128, 2 * D], F32))
        xn3Th = _es.enter_context(nc.sbuf_tensor("xn3Th", [128, 2 * D], BF16))
        xn3Tl = _es.enter_context(nc.sbuf_tensor("xn3Tl", [128, 2 * D], BF16))
        scr = _es.enter_context(nc.sbuf_tensor("scr3", [128, D], F32))
        scrD = _es.enter_context(nc.sbuf_tensor("scrD3", [128, D], F32))
        idt = _es.enter_context(nc.sbuf_tensor("idt4", [128, 128], F32))
        stat = _es.enter_context(nc.sbuf_tensor("stat3", [128, 16], F32))
        qf00 = _es.enter_context(nc.psum_tensor("qf00", [128, 512], F32))
        qf01 = _es.enter_context(nc.psum_tensor("qf01", [128, 512], F32))
        qf10 = _es.enter_context(nc.psum_tensor("qf10", [128, 512], F32))
        qf11 = _es.enter_context(nc.psum_tensor("qf11", [128, 512], F32))
        rt00 = _es.enter_context(nc.psum_tensor("rt00", [128, 512], F32))
        rt01 = _es.enter_context(nc.psum_tensor("rt01", [128, 512], F32))
        rt10 = _es.enter_context(nc.psum_tensor("rt10", [128, 512], F32))
        rt11 = _es.enter_context(nc.psum_tensor("rt11", [128, 512], F32))
        qfp = [[qf00, qf01], [qf10, qf11]]
        ptrp = [[rt00, rt01], [rt10, rt11]]

        entry_w = T.get("entry_waits", [])
        p_.put("sync", lambda s: s.dma_start(out=idt[:, :], in_=T["ident"].ap()),
               waits=entry_w, inc="dwl", inc_n=16)
        for fn_, ws_, inc_, incn_ in T.get("gps_pre", []):
            p_.put("gpsimd", fn_, waits=ws_, inc=inc_, inc_n=incn_)

        te_tr = [3 * t + 1 for t in range(n_tiles)]
        te_fh = [3 * t + 2 for t in range(n_tiles)]
        te_ffn = [3 * t + 3 for t in range(n_tiles)]
        ld_x = [32 * (t // 4 + 1) for t in range(n_tiles)]
        st_y = [16 * (t // 2 + 1) for t in range(n_tiles)]
        dv_x2 = [0] * n_tiles; dv_xn3 = [0] * n_tiles; dv_out = [0] * n_tiles
        dv_lo = [0] * n_tiles
        ac_st = [0] * n_tiles; ac_ev = [0] * n_tiles

        def slots(t):
            p = t % 2
            q4 = t % 4
            return dict(
                p=p,
                x1a=x1t[:, q4 * D:(q4 + 1) * D],
                oa=ot[:, (t % 8) * D:(t % 8 + 1) * D],
                x2a=x2b[:, q4 * D:(q4 + 1) * D],
                x3a=x3b[:, p * D:(p + 1) * D],
                xna=xn3[:, p * D:(p + 1) * D],
                xnha=xn3Th[:, p * D:(p + 1) * D],
                xnla=xn3Tl[:, p * D:(p + 1) * D],
                sx=stat[:, p * 8 + 0:p * 8 + 1],
                sq=stat[:, p * 8 + 1:p * 8 + 2],
                mu=stat[:, p * 8 + 2:p * 8 + 3],
                inv=stat[:, p * 8 + 3:p * 8 + 4],
                e2=stat[:, p * 8 + 4:p * 8 + 5],
                m2=stat[:, p * 8 + 5:p * 8 + 6])

        def front1(t):
            if t >= n_tiles:
                return
            S_ = slots(t)
            DL = "dl" + str(t % 4)

            xw = T.get("x1_waits", []) if t == 0 else []
            p_.put("sync",
                   lambda s, S_=S_, t=t: s.dma_start(
                       out=S_["x1a"], in_=T["x1s"][t * C:(t + 1) * C, :]),
                   waits=[("ddv", dv_x2[t - 4] if t >= 4 else None),
                          (DL, 32 * (t // 4) if t >= 4 else None)] + xw,
                   inc=DL, inc_n=16)

            if t % 4 == 0:
                batch = [tt for tt in range(t, t + 4) if tt < n_tiles]

                def old_(s, t=t, batch=batch):
                    rk = s.cc_rank(replica_groups=T["rg"])
                    for i in s.Switch(rk, NCORES):
                        for tt in batch:
                            if tt % H == 0:
                                T["o_half_wait"](s, i, tt // H)
                            s.dma_start(
                                out=slots(tt)["oa"],
                                in_=T["o_in_direct"](i, tt)).then_inc(
                                sems["dl" + str(tt % 4)], 16)
                    return None
                for tt in batch:
                    p_.ctr["dl" + str(tt % 4)] = (
                        p_.ctr.get("dl" + str(tt % 4), 0) + 16)
                p_.put("sync", old_,
                       waits=[("ddv", dv_x2[t - 5] if t >= 5 else None)]
                       + [("dl" + str(tt % 4),
                           32 * (tt // 4) if tt >= 4 else None)
                          for tt in batch])

            def mkx2(v, S_=S_, t=t):
                if t == 0:
                    v.drain()
                v.tensor_tensor(S_["x2a"], S_["x1a"], S_["oa"], op=ALU.add)
                v.drain()
                return v.tensor_scalar(scrD[:, :], S_["x2a"], 1.0, None,
                                       op0=ALU.mult, op1=ALU.add,
                                       accum_out=S_["sx"])
            dv_x2[t] = p_.put("vector", mkx2,
                              waits=[(DL, ld_x[t]),
                                     ("dac", ac_st[t - 2] if t >= 2 else None),
                                     ("ddv", dv_out[t - 4] if t >= 4 else None)],
                              inc="ddv")

        def front2a(t):
            if t >= n_tiles:
                return
            S_ = slots(t)
            def sqf(v, S_=S_):
                v.drain()
                return v.scalar_tensor_tensor(
                    scrD[:, :], S_["x2a"], 1.0, S_["x2a"],
                    op0=ALU.mult, op1=ALU.mult, accum_out=S_["sq"])
            p_.put("vector", sqf, inc="ddv")

            def mini(v, S_=S_):
                v.drain()
                v.tensor_tensor(S_["m2"], S_["sx"], S_["sx"], op=ALU.mult)
                v.tensor_scalar(S_["e2"], S_["sq"], 1.0 / D, EPS,
                                op0=ALU.mult, op1=ALU.add)
                v.tensor_scalar(S_["mu"], S_["sx"], 1.0 / D, None, op0=ALU.mult)
                v.drain()
                return v.scalar_tensor_tensor(S_["m2"], S_["m2"], -1.0 / (D * D),
                                              S_["e2"], op0=ALU.mult, op1=ALU.add)
            dv_m = p_.put("vector", mini, inc="ddv")

            def sqr(s, S_=S_):
                s.drain()
                return s.activation(S_["sq"], S_["m2"], AF.Sqrt)
            ac_st[t] = p_.put("scalar", sqr, waits=[("ddv", dv_m)], inc="dac")

        def front2b(t):
            if t >= n_tiles:
                return
            S_ = slots(t)

            def xn3f(v, S_=S_):
                v.reciprocal(S_["inv"], S_["sq"])
                v.drain()
                return v.tensor_scalar(S_["xna"], S_["x2a"], S_["mu"], S_["inv"],
                                       op0=ALU.subtract, op1=ALU.mult)
            dv_xn3[t] = p_.put(
                "vector", xn3f,
                waits=[("dac", ac_st[t]),
                       ("dte", te_tr[t - 2] if t >= 2 else None)],
                inc="ddv")

        def mid(t):
            S_ = slots(t)
            p = S_["p"]
            rtr = ptrp[p]

            def trs(te, S_=S_, rtr=rtr):
                last = None
                for kk in range(DCH):
                    last = te.transpose(
                        rtr[kk // 4][:, (kk % 4) * 128:(kk % 4 + 1) * 128],
                        S_["xna"][:, kk * 128:(kk + 1) * 128], idt[:, :])
                return last
            assert te_tr[t] == p_.put(
                "tensor", trs,
                waits=[("ddv", dv_xn3[t]), ("dwl", 16),
                       ("dac", ac_ev[t - 2] if t >= 2 else None),
                       ("ddv", dv_lo[t - 2] if t >= 2 else None)]
                + (T.get("entry_te_waits", []) if t == 0 else []),
                inc="dte")

            def evacs(s, S_=S_, rtr=rtr):
                s.copy(S_["xnha"][:, 0:512], rtr[0][:, :])
                return s.copy(S_["xnha"][:, 512:1024], rtr[1][:, :])
            ac_ev[t] = p_.put("scalar", evacs, waits=[("dte", te_tr[t])], inc="dac")

            def evlo(v, S_=S_, rtr=rtr):
                v.tensor_tensor(S_["xnla"][:, 0:512], rtr[0][:, :],
                                S_["xnha"][:, 0:512], op=ALU.subtract)
                return v.tensor_tensor(S_["xnla"][:, 512:1024], rtr[1][:, :],
                                       S_["xnha"][:, 512:1024], op=ALU.subtract)
            dv_lo[t] = p_.put("vector", evlo,
                              waits=[("dac", ac_ev[t]), ("dte", te_tr[t])],
                              inc="ddv")

            def ffn_hi(te, S_=S_, p=p):
                last = None
                for h in (0, 1):
                    for kk in range(DCH):
                        last = te.matmul(
                            qfp[p][h][:, :], S_["xnha"][:, kk * 128:(kk + 1) * 128],
                            sw16[:, kk * D + h * 512: kk * D + h * 512 + 512],
                            start=(kk == 0), stop=False)
                return last
            assert te_fh[t] == p_.put(
                "tensor", ffn_hi,
                waits=[("dac", ac_ev[t]),
                       ("ddv", dv_out[t - 2] if t >= 2 else None)],
                inc="dte")

            def ffn_lo(te, S_=S_, p=p):
                last = None
                for h in (0, 1):
                    for kk in range(DCH):
                        last = te.matmul(
                            qfp[p][h][:, :], S_["xnla"][:, kk * 128:(kk + 1) * 128],
                            sw16[:, kk * D + h * 512: kk * D + h * 512 + 512],
                            start=False, stop=(kk == DCH - 1))
                return last
            assert te_ffn[t] == p_.put(
                "tensor", ffn_lo, waits=[("ddv", dv_lo[t])], inc="dte")

        def back(t):
            S_ = slots(t)
            p = S_["p"]

            def outw(v, S_=S_, p=p):
                v.scalar_tensor_tensor(S_["x3a"][:, 0:512], qfp[p][0][:, :], 0.5,
                                       S_["x2a"][:, 0:512], op0=ALU.mult, op1=ALU.add)
                return v.scalar_tensor_tensor(S_["x3a"][:, 512:1024], qfp[p][1][:, :],
                                              0.5, S_["x2a"][:, 512:1024],
                                              op0=ALU.mult, op1=ALU.add)
            dv_out[t] = p_.put("vector", outw,
                               waits=[("dte", te_ffn[t]),
                                      ("sy" + str(p), st_y[t - 2] if t >= 2 else None)],
                               inc="ddv")
            assert st_y[t] == p_.put(
                "gpsimd",
                lambda g, S_=S_, t=t: g.dma_start(
                    out=T["y"][t * C:(t + 1) * C, :], in_=S_["x3a"]),
                waits=[("ddv", dv_out[t])], inc="sy" + str(p), inc_n=16)

        front1(0); front2a(0); front2b(0)
        front1(1)
        for t in range(n_tiles):
            mid(t)
            front2a(t + 1)
            front2b(t + 1)
            back(t)
            front1(t + 2)

        p_.put("gpsimd", lambda g: g.nop(),
               waits=[("sy0", p_.ctr.get("sy0", 0)), ("sy1", p_.ctr.get("sy1", 0))])
        if plan is None:
            realize(nc, p_, sems)


D_SEMS = ("dl0", "dl1", "dl2", "dl3", "dwl", "sy0", "sy1", "dte", "ddv", "dac")


GK_TILE = 2 * DCH * 128 * 128 + 2 * 128 * 128
NG = 8


def build_fused(n_tiles=NT):
    nc = bass.Bass("TRN2", target_bir_lowering=False, debug=False)
    S = n_tiles * C
    n_chunks = NCORES * n_tiles
    ng = min(NG, n_tiles)
    if n_tiles == 16:
        GSZ = [2] * 8
    else:
        GSZ = [n_tiles // ng] * ng
        GSZ[-1] += n_tiles - sum(GSZ)
    GOF = [sum(GSZ[:g]) for g in range(ng)]
    G_OF_T = []
    for g in range(ng):
        G_OF_T += [g] * GSZ[g]

    dr = {}
    dr["x"] = nc.dram_tensor("x", [S, D], F32, kind="ExternalInput")
    for nm in ("fwT", "thKT", "thVT", "thQT"):
        dr[nm] = nc.dram_tensor(nm, [D, D], F32, kind="ExternalInput")
    for nm in ("ident", "maskL", "maskU", "maskB"):
        dr[nm] = nc.dram_tensor(nm, [128, 128], F32, kind="ExternalInput")
    dr["w0t"] = nc.dram_tensor("w0t", [D, RS], F32, kind="ExternalInput")
    dr["b0s"] = nc.dram_tensor("b0s", [1, RS], F32, kind="ExternalInput")
    dr["y"] = nc.dram_tensor("y", [S, D], F32, kind="ExternalOutput")

    dr["x1"] = nc.dram_tensor("x1loc", [S, D], F32)
    dr["x1s"] = dr["x1"]
    gk = nc.dram_tensor("gk", [n_tiles, GK_TILE], F32)
    gk_all = [nc.dram_tensor("gk_all%d" % g, [NCORES * GSZ[g], GK_TILE], F32,
                             addr_space="Shared")
              for g in range(ng)]
    va_in = [nc.dram_tensor("va_in%d" % g, [NCORES, GSZ[g] * 128, RS], F32)
             for g in range(ng)]
    va_all = [nc.dram_tensor("va_all%d" % g, [NCORES * NCORES, GSZ[g] * 128, RS],
                             F32, addr_space="Shared")
              for g in range(ng)]
    va_my = nc.dram_tensor("va_my", [NCORES, S, RS], F32)
    oa_in = nc.dram_tensor("oa_in", [NCORES, S, RS], F32)
    nh = 2 if n_tiles >= 2 else 1
    H = n_tiles // nh
    oa_half = [nc.dram_tensor("oa_half%d" % q, [NCORES, H * 128, RS], F32,
                              addr_space="Shared")
               for q in range(NCORES * nh)]

    KQ = DCH * 128 * 128
    dr["kt_out"] = lambda t: bass.AP(gk, t * GK_TILE,
                                     [[128, 128], [128 * 128, DCH], [1, 128]])
    dr["qt_out"] = lambda t: bass.AP(gk, t * GK_TILE + KQ,
                                     [[128, 128], [128 * 128, DCH], [1, 128]])
    dr["mh_out"] = lambda t: bass.AP(gk, t * GK_TILE + 2 * KQ, [[128, 128], [1, 128]])
    dr["bh_out"] = lambda t: bass.AP(gk, t * GK_TILE + 2 * KQ + 128 * 128,
                                     [[128, 128], [1, 128]])
    def _vout(t):
        g = G_OF_T[t]
        return bass.AP(va_in[g], (t - GOF[g]) * 128 * RS,
                       [[RS, 128], [GSZ[g] * 128 * RS, NCORES], [1, RS]])
    dr["v_out"] = _vout

    def _gkrow(j):
        tile = j % n_tiles
        g = G_OF_T[tile]
        loc = tile - GOF[g]
        return g, ((j // n_tiles) * GSZ[g] + loc) * GK_TILE
    dr["kt_in"] = lambda j: bass.AP(gk_all[_gkrow(j)[0]], _gkrow(j)[1],
                                    [[128, 128], [128 * 128, DCH], [1, 128]])
    dr["qt_in"] = lambda j: bass.AP(gk_all[_gkrow(j)[0]], _gkrow(j)[1] + KQ,
                                    [[128, 128], [128 * 128, DCH], [1, 128]])
    dr["mh_in"] = lambda j: bass.AP(gk_all[_gkrow(j)[0]], _gkrow(j)[1] + 2 * KQ,
                                    [[128, 128], [1, 128]])
    dr["bh_in"] = lambda j: bass.AP(gk_all[_gkrow(j)[0]],
                                    _gkrow(j)[1] + 2 * KQ + 128 * 128,
                                    [[128, 128], [1, 128]])
    dr["v_in"] = lambda j: bass.AP(va_my, (j // n_tiles) * S * RS
                                   + (j % n_tiles) * 128 * RS, [[RS, 128], [1, RS]])
    dr["o_out"] = lambda j: bass.AP(oa_in, (j // n_tiles) * S * RS
                                    + (j % n_tiles) * 128 * RS, [[RS, 128], [1, RS]])
    dr["o_in_direct"] = lambda i, t: bass.AP(
        oa_half[i * nh + t // H], (t % H) * 128 * RS,
        [[RS, 128], [H * 128 * RS, NCORES], [1, RS]])

    rg = [list(range(NCORES))]
    sems_ref = [None]

    def tile_done_hook(p_, u, stor):
        if u + 1 not in [GOF[g] + GSZ[g] for g in range(ng)]:
            return
        g = G_OF_T[u]
        ws = [("bsv", stor["st_v"][u]), ("bsk", stor["st_kt"][u]),
              ("bsq", stor["st_qt"][u]), ("bsm", stor["st_mh"][u]),
              ("bsb", stor["st_bh"][u])]
        p_.put("gpsimd",
               lambda gps, g=g: gps.collective_compute(
                   "AllGather", ALU.bypass, replica_groups=rg,
                   ins=[gk[GOF[g]:GOF[g] + GSZ[g], :]], outs=[gk_all[g].ap()]),
               waits=ws, inc="cg%d" % g)
        p_.put("gpsimd",
               lambda gps, g=g: gps.collective_compute(
                   "AllGather", ALU.bypass, replica_groups=rg,
                   ins=[va_in[g].ap()], outs=[va_all[g].ap()]),
               inc="cv%d" % g)
    dr["tile_done_hook"] = tile_done_hook

    def mk_vex(g):
        def vex(s):
            if g >= 1:
                s.wait_ge(sems_ref[0]["vex"], 16 * NCORES * g)
            rk = s.cc_rank(replica_groups=rg)
            for i in s.Switch(rk, NCORES):
                for src in range(NCORES):
                    s.dma_start(
                        out=va_my[src, GOF[g] * 128:(GOF[g] + GSZ[g]) * 128, :],
                        in_=va_all[g][src * NCORES + i, :, :]).then_inc(
                        sems_ref[0]["vex"], 16)
            return None
        return vex

    dr2 = dict(dr)
    gps_at = {}
    gpa = {}
    spa = {}
    pos_map = ({0: 4, 1: 6, 2: 8, 3: 10, 4: 12, 5: 14}
               if n_tiles == 16 else {})
    for g in range(ng):
        item = (mk_vex(g), [("cv%d" % g, 1)], None, 1)
        if g in pos_map:
            gps_at.setdefault(pos_map[g], []).append(item)
        elif n_tiles == 16 and g == 6:
            gpa.setdefault(4, []).append(item)
        elif n_tiles == 16 and g == 7:
            gpa.setdefault(12, []).append(item)
        else:
            gpa.setdefault(max(GOF[g] - 4, 0), []).append(item)
    dr2["gps_at"] = gps_at
    dr2["gps_pre_at"] = gpa
    dr2["sync_pre_at"] = spa
    dr2["ld_waits"] = lambda j: [
        ("cg%d" % G_OF_T[j % n_tiles], 1),
        ("vex", 16 * NCORES * (G_OF_T[j % n_tiles] + 1))]

    def chunk_done_hook(p_, j, st_O):
        if (j + 1) % H != 0:
            return
        d = j // n_tiles
        half = (j % n_tiles) // H
        ws = [("so%d" % k, p_.ctr.get("so%d" % k, 0)) for k in range(8)]
        p_.put("gpsimd",
               lambda gps, d=d, half=half: gps.collective_compute(
                   "AllGather", ALU.bypass, replica_groups=rg,
                   ins=[oa_in[d:d + 1, half * H * 128:(half + 1) * H * 128, :]],
                   outs=[oa_half[d * nh + half].ap()]),
               waits=ws, inc="coh%d" % (d * nh + half))
    dr2["chunk_done_hook"] = chunk_done_hook

    dr3 = dict(dr)
    dr3["rg"] = rg
    dr3["o_half_wait"] = lambda s, i, hf: s.wait_ge(
        sems_ref[0]["coh%d" % (i * nh + hf)], 1)

    with ExitStack() as _es:
        extra = tuple(["cg%d" % g for g in range(ng)]
                      + ["cv%d" % g for g in range(ng)]
                      + ["coh%d" % q for q in range(NCORES * nh)] + ["vex"])
        sems = mk_sems(nc, _es, A_SEMS + B_SEMS + C_SEMS + D_SEMS + extra)
        sems_ref[0] = sems
        sw16g = _es.enter_context(nc.sbuf_tensor("sw16g", [128, DCH * D], BF16))
        dr2["sw16g"] = sw16g
        dr3["sw16g"] = sw16g
        plan = Plan()
        prev = emit_pass_ab(nc, sems, n_tiles, dr2, plan=plan)
        dr3["x1_waits"] = [("ax0", prev["ax0"]), ("ax1", prev["ax1"])]
        dr2["entry_waits"] = [("bte", prev["tek"])]
        dr2["w_trs0"] = [("adv", prev["epi"]),
                         ("bac", max(prev["qev"], prev["vev"]))]
        dr2["w_u0"] = [("bac", prev["kev"])]
        dr2["w_qs0"] = [("aac", prev["evy"]), ("adv", prev["ylo"])]
        dr2["w_g0"] = [("bac", prev["kev"])]
        dr2["w_upd0"] = [("bac", prev["lv"]), ("bdv", prev["pa"]),
                         ("aac", prev["evy"]), ("adv", prev["ylo"])]
        p2 = emit_phase2(nc, sems, n_chunks, dr2, plan=plan)
        dr3["entry_waits"] = [("cte", p2["cte"])] + p2["so"]
        dr3["entry_te_waits"] = [("cdv", p2["cdv"]), ("cac", p2["cac"])]
        emit_phase3(nc, sems, n_tiles, dr3, plan=plan)
        realize(nc, plan, sems)
    return nc


def kernel(L0_attn_in_seq, L0_attn_theta_K, L0_attn_theta_V, L0_attn_theta_Q,
           L0_attn_W0, L0_attn_b0, L0_ffn_weight):
    x = np.ascontiguousarray(np.asarray(L0_attn_in_seq, np.float32).reshape(S_TOT, D))
    thK = np.asarray(L0_attn_theta_K, np.float32)
    thV = np.asarray(L0_attn_theta_V, np.float32)
    thQ = np.asarray(L0_attn_theta_Q, np.float32)
    W0 = np.asarray(L0_attn_W0, np.float32)
    b0 = np.asarray(L0_attn_b0, np.float32)
    fw = np.asarray(L0_ffn_weight, np.float32)

    fwT = np.ascontiguousarray(fw.T)
    thKT = np.ascontiguousarray(thK.T)
    thVT = np.ascontiguousarray(thV.T)
    thQT = np.ascontiguousarray(thQ.T)
    ident, maskL, maskU, maskB = np_consts()

    nc = build_fused()
    in_maps = []
    for c in range(NCORES):
        in_maps.append(dict(
            x=np.ascontiguousarray(x[c * TPC:(c + 1) * TPC]),
            fwT=fwT, thKT=thKT, thVT=thVT, thQT=thQT,
            ident=ident, maskL=maskL, maskU=maskU, maskB=maskB,
            w0t=np.ascontiguousarray(W0[c * RS:(c + 1) * RS, :].T),
            b0s=np.ascontiguousarray(b0[c * RS:(c + 1) * RS]).reshape(1, RS)))
    res = run_bass_kernel_spmd(nc, in_maps, core_ids=list(range(NCORES))).results
    y = np.concatenate([r["y"] for r in res], 0)
    return y.reshape(1, S_TOT, D).astype(np.float32)


# version 43
# speedup vs baseline: 1.0162x; 1.0162x over previous
import numpy as np
from contextlib import ExitStack
import concourse.bass as bass
import concourse.mybir as mybir
from concourse.bass_utils import run_bass_kernel_spmd

F32 = mybir.dt.float32
BF16 = mybir.dt.bfloat16
AF = mybir.ActivationFunctionType
ALU = mybir.AluOpType
AX = mybir.AxisListType

D = 1024
S_TOT = 16384
NCORES = 8
C = 128
TPC = S_TOT // NCORES
NT = TPC // C
NCH = S_TOT // C
RS = D // NCORES
LR = 0.01
CCONST = 2.0 / D
EPS = 1e-5
DCH = D // 128

ENGS = ("tensor", "vector", "scalar", "sync", "gpsimd")


class Plan:
    def __init__(self):
        self.items = {e: [] for e in ENGS}
        self.ctr = {}

    def put(self, eng, fn, waits=(), inc=None, inc_n=1):
        tgt = None
        if inc is not None:
            self.ctr[inc] = self.ctr.get(inc, 0) + inc_n
            tgt = self.ctr[inc]
        ws = tuple((s, v) for (s, v) in waits if v is not None and v > 0)
        self.items[eng].append((ws, fn, inc, inc_n))
        return tgt


def realize(nc, p_, sems):
    with nc.Block() as blk:
        for engname in ENGS:
            items = p_.items[engname]
            if not items:
                continue

            def mk(items):
                def body(eng):
                    for ws, fn, inc, inc_n in items:
                        for (s, v) in ws:
                            eng.wait_ge(sems[s], v)
                        last = fn(eng)
                        if inc is not None:
                            if last is None:
                                last = eng.nop()
                            last.then_inc(sems[inc], inc_n)
                return body

            getattr(blk, engname)(mk(items))


def mk_sems(nc, es, names):
    return {n: es.enter_context(nc.semaphore(n)) for n in names}


def np_consts():
    ident = np.eye(128, dtype=np.float32)
    maskL = np.tril(np.ones((128, 128), np.float32), -1)
    maskU = np.triu(np.ones((128, 128), np.float32), 1)
    maskB = np.triu(np.ones((128, 128), np.float32), 0)
    return ident, maskL, maskU, maskB


def _wT_ap(w_d):
    return bass.AP(w_d, 0, [[D, 128], [128 * D, DCH], [1, D]])


def emit_pass_ab(nc, sems, n_tiles, T, plan=None):
    p_ = plan if plan is not None else Plan()
    with ExitStack() as _es:
        wscr = _es.enter_context(nc.sbuf_tensor("wscr", [128, 4 * D], F32))
        sw16 = T["sw16g"]
        thKh = _es.enter_context(nc.sbuf_tensor("thKh", [128, DCH * D], BF16))
        thKl = _es.enter_context(nc.sbuf_tensor("thKl", [128, DCH * D], BF16))
        thVh = _es.enter_context(nc.sbuf_tensor("thVh", [128, DCH * D], BF16))
        thVl = _es.enter_context(nc.sbuf_tensor("thVl", [128, DCH * D], BF16))
        thQh = _es.enter_context(nc.sbuf_tensor("thQh", [128, DCH * D], BF16))
        thQl = _es.enter_context(nc.sbuf_tensor("thQl", [128, DCH * D], BF16))
        xb = _es.enter_context(nc.sbuf_tensor("xb", [128, 2 * D], F32))
        x1b = _es.enter_context(nc.sbuf_tensor("x1b", [128, 2 * D], F32))
        xn2s = _es.enter_context(nc.sbuf_tensor("xn2s", [128, 2 * D], F32))
        xn = _es.enter_context(nc.sbuf_tensor("xn", [128, D], F32))
        xnTh = _es.enter_context(nc.sbuf_tensor("xnTh", [128, D], BF16))
        xnTl = _es.enter_context(nc.sbuf_tensor("xnTl", [128, D], BF16))
        scrA = _es.enter_context(nc.sbuf_tensor("scrA", [128, D], F32))
        scrS = _es.enter_context(nc.sbuf_tensor("scrS", [128, D], F32))
        yTh = _es.enter_context(nc.sbuf_tensor("yTh", [128, D], BF16))
        yTl = _es.enter_context(nc.sbuf_tensor("yTl", [128, D], BF16))
        KQt = _es.enter_context(nc.sbuf_tensor("KQt", [128, D], F32))
        VtB = _es.enter_context(nc.sbuf_tensor("VtB", [128, D], F32))
        KTb = _es.enter_context(nc.sbuf_tensor("KTb", [128, D], F32))
        QTb = _es.enter_context(nc.sbuf_tensor("QTb", [128, D], F32))
        idt = _es.enter_context(nc.sbuf_tensor("idt", [128, 128], F32))
        mkL = _es.enter_context(nc.sbuf_tensor("mkL", [128, 128], F32))
        mkU = _es.enter_context(nc.sbuf_tensor("mkU", [128, 128], F32))
        mkB = _es.enter_context(nc.sbuf_tensor("mkB", [128, 128], F32))
        Atm = _es.enter_context(nc.sbuf_tensor("Atm", [128, 128], F32))
        Btm = _es.enter_context(nc.sbuf_tensor("Btm", [128, 128], F32))
        Xb = _es.enter_context(nc.sbuf_tensor("Xb", [128, 2 * 256], F32))
        Yb = _es.enter_context(nc.sbuf_tensor("Yb", [128, 2 * 256], F32))
        Pb = _es.enter_context(nc.sbuf_tensor("Pb", [128, 2 * 128], F32))
        MhS = _es.enter_context(nc.sbuf_tensor("MhS", [128, 128], F32))
        BhS = _es.enter_context(nc.sbuf_tensor("BhS", [128, 128], F32))
        stat = _es.enter_context(nc.sbuf_tensor("stat", [128, 32], F32))
        pj0 = _es.enter_context(nc.psum_tensor("pj0", [128, 512], F32))
        pj1 = _es.enter_context(nc.psum_tensor("pj1", [128, 512], F32))
        pj2 = _es.enter_context(nc.psum_tensor("pj2", [128, 512], F32))
        pj3 = _es.enter_context(nc.psum_tensor("pj3", [128, 512], F32))
        qtr0 = _es.enter_context(nc.psum_tensor("qtr0", [128, 512], F32))
        qtr1 = _es.enter_context(nc.psum_tensor("qtr1", [128, 512], F32))
        pm_ = _es.enter_context(nc.psum_tensor("pmB", [128, 512], F32))
        pab = _es.enter_context(nc.psum_tensor("pAB", [128, 512], F32))
        pX2 = pm_[:, 128:256]
        pY2 = pm_[:, 256:384]
        pP2 = pm_[:, 384:512]
        pA = pab[:, 0:128]
        pB = pab[:, 128:256]
        qtr = [qtr0, qtr1]

        for nm, dst in (("ident", idt), ("maskL", mkL), ("maskU", mkU), ("maskB", mkB)):
            p_.put("sync", lambda s, dst=dst, nm=nm: s.dma_start(out=dst[:, :], in_=T[nm].ap()),
                   inc="bwm", inc_n=16)
        BWM = 16 * 4
        for t0_ in range(min(2, n_tiles)):
            p_.put("sync",
                   lambda s, t=t0_: s.dma_start(
                       out=xb[:, (t % 2) * D:(t % 2 + 1) * D],
                       in_=T["x"][t * C:(t + 1) * C, :]),
                   inc="al" + str(t0_ % 2), inc_n=16)
        wgates = {}

        def emit_wrounds():
            slots8 = [wscr[:, k * D:(k + 1) * D] for k in range(4)] + [
                KQt[:, :], VtB[:, :], KTb[:, :], QTb[:, :]]
            qcnt = {}
            names = ["fwT", "thVT", "thKT", "thQT"]

            def wdma(r, widx, blk):
                qeng, qsem = (("sync", "bw%d" % (r % 8)) if r % 2 == 0
                              else ("gpsimd", "bw%d" % (r % 8)))
                buf = slots8[r % 8]

                def dma(s, widx=widx, blk=blk, buf=buf):
                    return s.dma_start(out=buf, in_=bass.AP(
                        T[names[widx]], blk * 128 * D, [[D, 128], [1, D]]))
                if r < 8:
                    ws = []
                elif r < 16:
                    ws = [("bac", r - 8 + 1)]
                elif r < 24:
                    ws = [("bdv", r - 16 + 1)]
                else:
                    ws = [("bdv", r - 24 + 9)]
                qcnt[qsem] = qcnt.get(qsem, 0) + 16
                p_.put(qeng, dma, waits=ws, inc=qsem, inc_n=16)
                return (qsem, qcnt[qsem])

            dsem = [wdma(r, 0, r) for r in range(DCH)]
            for blk in range(DCH):
                ac = p_.put("scalar",
                            lambda s, off=blk * D, buf=slots8[blk]: s.sign(
                                sw16[:, off:off + D], buf),
                            waits=[dsem[blk]], inc="bac")
                assert ac == blk + 1
            wgates["AC_SW"] = DCH
            def wsplit(widx, dsth, dstl, ds, blk):
                off = blk * D
                buf = slots8[(widx * DCH + blk) % 8]
                ac = p_.put("scalar",
                            lambda s, dsth=dsth, off=off, buf=buf: s.copy(
                                dsth[:, off:off + D], buf),
                            waits=[ds], inc="bac")
                dv = p_.put(
                    "vector",
                    lambda v, dsth=dsth, dstl=dstl, off=off, buf=buf:
                    v.tensor_tensor(dstl[:, off:off + D], buf,
                                    dsth[:, off:off + D], op=ALU.subtract),
                    waits=[("bac", ac)], inc="bdv")
                assert dv == (widx - 1) * DCH + blk + 1
                return dv

            dsems = {}
            for blk in range(DCH):
                dsems[(1, blk)] = wdma(DCH + blk, 1, blk)
                wgates["DV_WV"] = wsplit(1, thVh, thVl, dsems[(1, blk)], blk)
            for widx in (2, 3):
                for blk in range(DCH):
                    dsems[(widx, blk)] = wdma(widx * DCH + blk, widx, blk)

            def emit_wrounds_kq():
                for widx, (dsth, dstl) in ((2, (thKh, thKl)), (3, (thQh, thQl))):
                    for blk in range(DCH):
                        wgates[{2: "DV_WK", 3: "DV_WQ"}[widx]] = wsplit(
                            widx, dsth, dstl, dsems[(widx, blk)], blk)
            wgates["kq"] = emit_wrounds_kq

        Z = lambda: [0] * n_tiles
        ld_x = [16 * (t // 2 + 1) for t in range(n_tiles)]
        st_x1 = [16 * (t // 2 + 1) for t in range(n_tiles)]
        dv_s1 = Z(); dv_xi1 = Z(); dv_epi = Z(); dv_s2 = Z(); dv_xi2 = Z()
        dv_lo1 = Z(); dv_ylo = Z(); dv_Vev = Z(); dv_A0 = Z(); dv_Mh = Z(); dv_Bh = Z()
        dv_Pa = [[0] * 7 for _ in range(n_tiles)]
        ac_x1 = Z(); ac_x2 = Z(); ac_m1 = Z(); ac_m2 = Z()
        ac_s1 = Z(); ac_s2 = Z()
        ac_ev1 = Z(); ac_evy = Z(); ac_Kev = Z(); ac_KTev = Z()
        ac_Qev = Z(); ac_QTev = Z(); ac_atm = Z()
        ac_lv = [[0] * 7 for _ in range(n_tiles)]
        te_tr1 = Z(); te_ffn = Z(); te_tr2 = Z(); te_V = Z(); te_K = Z()
        te_Ktr = Z(); te_Q = Z(); te_Qtr = Z(); te_A = Z(); te_B = Z()
        te_sq = [[0] * 7 for _ in range(n_tiles)]
        te_P = [[0] * 7 for _ in range(n_tiles)]
        st_v = Z(); st_kt = Z(); st_qt = Z(); st_mh = Z(); st_bh = Z()

        def slots(t):
            p = t % 2
            b1 = p * 16
            b2 = p * 16 + 8
            d = dict(
                p=p,
                xa=xb[:, p * D:(p + 1) * D],
                x1a=x1b[:, p * D:(p + 1) * D],
                xn2a=xn2s[:, p * D:(p + 1) * D])
            for nm, off in (("sx", 0), ("sq", 1), ("mu", 2), ("inv", 3),
                            ("e2", 4), ("m2", 5)):
                d[nm] = stat[:, b1 + off:b1 + off + 1]
                d[nm + "2"] = stat[:, b2 + off:b2 + off + 1]
            return d

        def sum_dve(src_ap, sx):
            def f(v):
                v.drain()
                return v.tensor_scalar(scrA[:, :], src_ap, 1.0, None,
                                       op0=ALU.mult, op1=ALU.add, accum_out=sx)
            return f

        def sq_act(src_ap, sq):
            def f(s):
                s.drain()
                return s.activation(scrS[:, :], src_ap, AF.Square, accum_out=sq)
            return f

        def mini_item(sx, sq, mu, e2, m2):
            def f(v):
                v.drain()
                v.tensor_tensor(m2, sx, sx, op=ALU.mult)
                v.tensor_scalar(e2, sq, 1.0 / D, EPS, op0=ALU.mult, op1=ALU.add)
                v.tensor_scalar(mu, sx, -1.0 / D, None, op0=ALU.mult)
                v.drain()
                return v.scalar_tensor_tensor(m2, m2, -1.0 / (D * D), e2,
                                              op0=ALU.mult, op1=ALU.add)
            return f

        def sqrt_item(m2, sq):
            def f(s):
                s.drain()
                return s.activation(sq, m2, AF.Sqrt)
            return f

        def inv_item(inv, sq, mu):
            def f(v):
                v.reciprocal(inv, sq)
                v.drain()
                return v.tensor_tensor(mu, mu, inv, op=ALU.mult)
            return f

        def trs(te, srcf):
            last = None
            for kk in range(DCH):
                last = te.transpose(qtr[kk // 4][:, (kk % 4) * 128:(kk % 4 + 1) * 128],
                                    srcf[:, kk * 128:(kk + 1) * 128], idt[:, :])
            return last

        def ev_hi(s, dst):
            s.copy(dst[:, 0:512], qtr0[:, :])
            return s.copy(dst[:, 512:1024], qtr1[:, :])

        def ev_lo(v, dsth, dstl):
            v.tensor_tensor(dstl[:, 0:512], qtr0[:, :], dsth[:, 0:512],
                            op=ALU.subtract)
            return v.tensor_tensor(dstl[:, 512:1024], qtr1[:, :],
                                   dsth[:, 512:1024], op=ALU.subtract)

        def ev_f32(s, dst, pja, pjb):
            s.copy(dst[:, 0:512], pja[:, :])
            return s.copy(dst[:, 512:1024], pjb[:, :])

        def proj_hi(te, xh, wh, pja, pjb, stop=False):
            last = None
            for h, pjx in ((0, pja), (1, pjb)):
                for kk in range(DCH):
                    last = te.matmul(
                        pjx[:, :], xh[:, kk * 128:(kk + 1) * 128],
                        wh[:, kk * D + h * 512: kk * D + h * 512 + 512],
                        start=(kk == 0), stop=(stop and kk == DCH - 1))
            return last

        def proj_rest(te, xh, xl, wh, wl, pja, pjb):
            last = None
            for h, pjx in ((0, pja), (1, pjb)):
                for pi, (xp, wp) in enumerate(((xl, wh), (xh, wl))):
                    for kk in range(DCH):
                        last = te.matmul(
                            pjx[:, :], xp[:, kk * 128:(kk + 1) * 128],
                            wp[:, kk * D + h * 512: kk * D + h * 512 + 512],
                            start=False, stop=(pi == 1 and kk == DCH - 1))
            return last

        def chain_sq(u, k):
            if u < 0 or u >= n_tiles:
                return
            q = u % 2
            xo = Xb[:, q * 256 + ((k - 1) % 2) * 128: q * 256 + ((k - 1) % 2) * 128 + 128]
            yo = Yb[:, q * 256 + ((k - 1) % 2) * 128: q * 256 + ((k - 1) % 2) * 128 + 128]
            xn_ = Xb[:, q * 256 + (k % 2) * 128: q * 256 + (k % 2) * 128 + 128]
            yn_ = Yb[:, q * 256 + (k % 2) * 128: q * 256 + (k % 2) * 128 + 128]

            if k == 1:
                w_dv = dv_Pa[u - 1][6] if u >= 1 else dv_A0[u]
                w_ac = ac_lv[u - 1][6] if u >= 1 else None
            else:
                w_dv = dv_Pa[u][k - 1]
                w_ac = ac_lv[u][k - 1]

            def sq(te, xo=xo, yo=yo):
                te.matmul(pX2, yo, xo, start=True, stop=True)
                return te.matmul(pY2, xo, yo, start=True, stop=True)
            te_sq[u][k] = p_.put("tensor", sq,
                                 waits=[("bdv", w_dv), ("bac", w_ac)], inc="bte")

            def lev(s, xn_=xn_, yn_=yn_):
                s.copy(xn_, pX2)
                return s.copy(yn_, pY2)
            ac_lv[u][k] = p_.put("scalar", lev, waits=[("bte", te_sq[u][k])], inc="bac")

        def chain_p(u, k):
            if u < 0 or u >= n_tiles:
                return
            q = u % 2
            xn_ = Xb[:, q * 256 + (k % 2) * 128: q * 256 + (k % 2) * 128 + 128]
            Pq = Pb[:, q * 128:(q + 1) * 128]
            w_dv = (dv_Pa[u - 1][6] if u >= 1 else dv_A0[u]) if k == 1 \
                else dv_Pa[u][k - 1]
            te_P[u][k] = p_.put(
                "tensor",
                lambda te, xn_=xn_, Pq=Pq: te.matmul(pP2, xn_, Pq, start=True, stop=True),
                waits=[("bac", ac_lv[u][k]), ("bdv", w_dv)], inc="bte")
            dv_Pa[u][k] = p_.put(
                "vector",
                lambda v, Pq=Pq: v.tensor_tensor(Pq, Pq, pP2, op=ALU.add),
                waits=[("bte", te_P[u][k])], inc="bdv")

        def chain_mh(u):
            if u < 0 or u >= n_tiles:
                return
            q = u % 2
            Pq = Pb[:, q * 128:(q + 1) * 128]

            def mhf(s, Pq=Pq):
                return s.activation(MhS[:, :], Pq, AF.Copy,
                                    scale=-(LR * CCONST))
            dv_Mh[u] = p_.put("scalar", mhf,
                              waits=[("bdv", dv_Pa[u][6]),
                                     ("bsm", st_mh[u - 1] if u >= 1 else None)],
                              inc="bac")
            st_mh[u] = p_.put(
                "gpsimd",
                lambda g, u=u: g.dma_start(out=T["mh_out"](u), in_=MhS[:, :]),
                waits=[("bac", dv_Mh[u])], inc="bsm", inc_n=16)
            hk = T.get("tile_done_hook")
            if hk:
                hk(p_, u, dict(st_v=st_v, st_kt=st_kt, st_qt=st_qt,
                               st_mh=st_mh, st_bh=st_bh))

        def fA(t):
            if t >= n_tiles:
                return
            S_ = slots(t)
            AL = "al" + str(S_["p"])
            if t >= 2:
                assert ld_x[t] == p_.put(
                    "sync",
                    lambda s, xa=S_["xa"], t=t: s.dma_start(
                        out=xa, in_=T["x"][t * C:(t + 1) * C, :]),
                    waits=[("adv", dv_epi[t - 2])],
                    inc=AL, inc_n=16)
            dv_s1[t] = p_.put("vector", sum_dve(S_["xa"], S_["sx"]),
                              waits=[(AL, ld_x[t]),
                                     ("aac", ac_m1[t - 2] if t >= 2 else None)],
                              inc="adv")

            def sq1f(v, S_=S_):
                v.drain()
                return v.scalar_tensor_tensor(
                    scrA[:, :], S_["xa"], 1.0, S_["xa"],
                    op0=ALU.mult, op1=ALU.mult, accum_out=S_["sq"])
            ac_s1[t] = 0
            p_.put("vector", sq1f,
                   waits=[("adv", dv_xi1[t - 2] if t >= 2 else None)],
                   inc="adv")

        def fB(t):
            if t >= n_tiles:
                return
            S_ = slots(t)
            dv_m = p_.put("vector",
                          mini_item(S_["sx"], S_["sq"], S_["mu"], S_["e2"], S_["m2"]),
                          waits=[("adv", dv_s1[t])], inc="adv")
            ac_m1[t] = p_.put("scalar", sqrt_item(S_["m2"], S_["sq"]),
                              waits=[("adv", dv_m)], inc="aac")
            dv_xi1[t] = p_.put("vector", inv_item(S_["inv"], S_["sq"], S_["mu"]),
                               waits=[("aac", ac_m1[t])], inc="adv")

        def fC(t):
            if t >= n_tiles:
                return
            S_ = slots(t)
            ac_x1[t] = p_.put(
                "scalar",
                lambda s, S_=S_: s.activation(xn[:, :], S_["xa"], AF.Identity,
                                              scale=S_["inv"], bias=S_["mu"]),
                waits=[("adv", dv_xi1[t]),
                       ("ate", te_tr1[t - 1] if t >= 1 else None)],
                inc="aac")

        fA(0); fB(0); fC(0)
        fA(1); fB(1)
        emit_wrounds()
        AC_SW = wgates["AC_SW"]
        DV_WV = wgates["DV_WV"]
        for t in range(n_tiles + 1):
            u = t - 1
            for fn_, ws_, inc_, incn_ in T.get("ac_at", {}).get(t, []):
                p_.put("scalar", fn_, waits=ws_, inc=inc_, inc_n=incn_)
            if t < n_tiles:
                S_ = slots(t)
                p = S_["p"]

                te_tr1[t] = p_.put("tensor", lambda te: trs(te, xn),
                                   waits=[("aac", ac_x1[t]), ("bwm", BWM),
                                          ("bac", ac_QTev[t - 1] if t >= 1 else None)],
                                   inc="ate")
                ac_ev1[t] = p_.put("scalar", lambda s: ev_hi(s, xnTh),
                                   waits=[("ate", te_tr1[t])], inc="aac")
                dv_lo1[t] = p_.put("vector", lambda v: ev_lo(v, xnTh, xnTl),
                                   waits=[("aac", ac_ev1[t]), ("ate", te_tr1[t])],
                                   inc="adv")

                def ffn_p(te, xp, pjx, h, start, stop):
                    last = None
                    for kk in range(DCH):
                        last = te.matmul(
                            pjx[:, :], xp[:, kk * 128:(kk + 1) * 128],
                            sw16[:, kk * D + h * 512: kk * D + h * 512 + 512],
                            start=(start and kk == 0),
                            stop=(stop and kk == DCH - 1))
                    return last
                p_.put("tensor",
                       lambda te: (ffn_p(te, xnTh, pj0, 0, True, False),
                                   ffn_p(te, xnTh, pj1, 1, True, False))[-1],
                       waits=[("aac", ac_ev1[t]),
                              ("bac", AC_SW if t == 0 else None),
                              ("adv", dv_epi[t - 1] if t >= 1 else None),
                              ("bac", ac_Qev[t - 1] if t >= 1 else None)],
                       inc="bte")
                te_f0 = p_.put("tensor",
                               lambda te: ffn_p(te, xnTl, pj0, 0, False, True),
                               waits=[("adv", dv_lo1[t])], inc="bte")
                te_ffn[t] = p_.put("tensor",
                                   lambda te: ffn_p(te, xnTl, pj1, 1, False, True),
                                   inc="bte")

                ha = stat[:, p * 16 + 6:p * 16 + 7]
                hb = stat[:, p * 16 + 7:p * 16 + 8]

                def epi0(v, S_=S_, ha=ha):
                    v.drain()
                    return v.scalar_tensor_tensor(
                        S_["x1a"][:, 0:512], pj0[:, :], 0.5, S_["xa"][:, 0:512],
                        op0=ALU.mult, op1=ALU.add, accum_out=ha)
                dv_e0 = p_.put(
                    "vector", epi0,
                    waits=[("bte", te_f0),
                           ("ax" + str(p), st_x1[t - 2] if t >= 2 else None)],
                    inc="adv")

                def epi1(v, S_=S_, ha=ha, hb=hb):
                    v.scalar_tensor_tensor(S_["x1a"][:, 512:1024], pj1[:, :],
                                           0.5, S_["xa"][:, 512:1024],
                                           op0=ALU.mult, op1=ALU.add, accum_out=hb)
                    v.drain()
                    return v.tensor_tensor(S_["sx2"], ha, hb, op=ALU.add)
                dv_epi[t] = p_.put(
                    "vector", epi1,
                    waits=[("bte", te_ffn[t])],
                    inc="adv")
                assert st_x1[t] == p_.put(
                    "gpsimd",
                    lambda g, S_=S_, t=t: g.dma_start(
                        out=T["x1"][t * C:(t + 1) * C, :], in_=S_["x1a"]),
                    waits=[("adv", dv_epi[t])], inc="ax" + str(p), inc_n=16)

                for fn_, ws_, inc_, incn_ in T.get("gps_at", {}).get(t, []):
                    p_.put("gpsimd", fn_, waits=ws_, inc=inc_, inc_n=incn_)

                fC(t + 1)

                def sq2f(v, S_=S_):
                    v.drain()
                    return v.scalar_tensor_tensor(
                        scrA[:, :], S_["x1a"], 1.0, S_["x1a"],
                        op0=ALU.mult, op1=ALU.mult, accum_out=S_["sq2"])
                p_.put("vector", sq2f,
                       waits=[("adv", dv_xi2[t - 2] if t >= 2 else None)],
                       inc="adv")

                def mini2(v, S_=S_):
                    v.drain()
                    v.tensor_tensor(S_["m22"], S_["sx2"], S_["sx2"], op=ALU.mult)
                    v.tensor_scalar(S_["e22"], S_["sq2"], 1.0 / D, EPS,
                                    op0=ALU.mult, op1=ALU.add)
                    v.tensor_scalar(S_["mu2"], S_["sx2"], 1.0 / D, None,
                                    op0=ALU.mult)
                    v.drain()
                    return v.scalar_tensor_tensor(S_["m22"], S_["m22"],
                                                  -1.0 / (D * D), S_["e22"],
                                                  op0=ALU.mult, op1=ALU.add)
                dv_m2 = p_.put("vector", mini2, waits=[], inc="adv")
                ac_m2[t] = p_.put("scalar", sqrt_item(S_["m22"], S_["sq2"]),
                                  waits=[("adv", dv_m2)], inc="aac")

                def invapp2(v, S_=S_):
                    v.reciprocal(S_["inv2"], S_["sq2"])
                    v.drain()
                    return v.tensor_scalar(S_["xn2a"], S_["x1a"], S_["mu2"],
                                           S_["inv2"], op0=ALU.subtract,
                                           op1=ALU.mult)
                dv_xi2[t] = p_.put("vector", invapp2,
                                   waits=[("aac", ac_m2[t])], inc="adv")

                te_tr2[t] = p_.put("tensor", lambda te, S_=S_: trs(te, S_["xn2a"]),
                                   waits=[("adv", dv_xi2[t]), ("aac", ac_ev1[t]),
                                          ("adv", dv_lo1[t])],
                                   inc="ate")
                ac_evy[t] = p_.put("scalar", lambda s: ev_hi(s, yTh),
                                   waits=[("ate", te_tr2[t])], inc="aac")
                dv_ylo[t] = p_.put("vector", lambda v: ev_lo(v, yTh, yTl),
                                   waits=[("aac", ac_evy[t]), ("ate", te_tr2[t])],
                                   inc="adv")

                if t == 0:
                    wgates["kq"]()
                    DV_WK = wgates["DV_WK"]
                    DV_WQ = wgates["DV_WQ"]

                chain_sq(u, 1)

                p_.put("tensor", lambda te: proj_hi(te, yTh, thVh, pj0, pj1),
                       waits=[("aac", ac_evy[t]), ("adv", dv_epi[t]),
                              ("bdv", DV_WV if t == 0 else None)],
                       inc="bte")
                te_V[t] = p_.put("tensor",
                                 lambda te: proj_rest(te, yTh, yTl, thVh, thVl,
                                                      pj0, pj1),
                                 waits=[("adv", dv_ylo[t])], inc="bte")
                dv_Vev[t] = p_.put(
                    "scalar", lambda s: ev_f32(s, VtB, pj0, pj1),
                    waits=[("bte", te_V[t]),
                           ("bdv", 22 if t == 0 else None),
                           ("bsv", st_v[t - 1] if t >= 1 else None)],
                    inc="bac")
                st_v[t] = p_.put(
                    "gpsimd",
                    lambda g, t=t: g.dma_start(out=T["v_out"](t), in_=VtB[:, :]),
                    waits=[("bac", dv_Vev[t])], inc="bsv", inc_n=16)

                chain_p(u, 1)

                p_.put("tensor", lambda te: proj_hi(te, yTh, thKh, pj2, pj3),
                       waits=[("bdv", DV_WK if t == 0 else None)], inc="bte")
                te_K[t] = p_.put("tensor",
                                 lambda te: proj_rest(te, yTh, yTl, thKh, thKl,
                                                      pj2, pj3),
                                 inc="bte")
                ac_Kev[t] = p_.put("scalar", lambda s: ev_f32(s, KQt, pj2, pj3),
                                   waits=[("bte", te_K[t]),
                                          ("bdv", 21 if t == 0 else None)],
                                   inc="bac")
                chain_sq(u, 2)
                te_Ktr[t] = p_.put("tensor", lambda te: trs(te, KQt),
                                   waits=[("bac", ac_Kev[t]),
                                          ("adv", dv_ylo[t])], inc="bte")
                ac_KTev[t] = p_.put("scalar", lambda s: ev_hi(s, KTb),
                                    waits=[("bte", te_Ktr[t]),
                                           ("bdv", 23 if t == 0 else None),
                                           ("bsk", st_kt[t - 1] if t >= 1 else None)],
                                    inc="bac")
                st_kt[t] = p_.put(
                    "gpsimd",
                    lambda g, t=t: g.dma_start(out=T["kt_out"](t), in_=KTb[:, :]),
                    waits=[("bac", ac_KTev[t])], inc="bsk", inc_n=16)
                chain_p(u, 2)

                p_.put("tensor", lambda te: proj_hi(te, yTh, thQh, pj0, pj1),
                       waits=[("bac", dv_Vev[t]),
                              ("bdv", DV_WQ if t == 0 else None)], inc="bte")
                te_Q[t] = p_.put("tensor",
                                 lambda te: proj_rest(te, yTh, yTl, thQh, thQl,
                                                      pj0, pj1),
                                 inc="bte")
                ac_Qev[t] = p_.put("scalar", lambda s: ev_f32(s, KQt, pj0, pj1),
                                   waits=[("bte", te_Q[t]), ("bte", te_Ktr[t])],
                                   inc="bac")
                chain_sq(u, 3)
                te_Qtr[t] = p_.put("tensor", lambda te: trs(te, KQt),
                                   waits=[("bac", ac_Qev[t])], inc="bte")
                ac_QTev[t] = p_.put("scalar", lambda s: ev_hi(s, QTb),
                                    waits=[("bte", te_Qtr[t]),
                                           ("bdv", 24 if t == 0 else None),
                                           ("bsq", st_qt[t - 1] if t >= 1 else None)],
                                    inc="bac")
                st_qt[t] = p_.put(
                    "gpsimd",
                    lambda g, t=t: g.dma_start(out=T["qt_out"](t), in_=QTb[:, :]),
                    waits=[("bac", ac_QTev[t])], inc="bsq", inc_n=16)
                chain_p(u, 3)

                def amm(te):
                    last = None
                    for kk in range(DCH):
                        last = te.matmul(pA, KTb[:, kk * 128:(kk + 1) * 128],
                                         KTb[:, kk * 128:(kk + 1) * 128],
                                         start=(kk == 0), stop=(kk == DCH - 1))
                    return last
                te_A[t] = p_.put("tensor", amm,
                                 waits=[("bac", ac_KTev[t]),
                                        ("bdv", dv_Bh[t - 1] if t >= 1 else None)],
                                 inc="bte")
                ac_atm[t] = p_.put(
                    "scalar",
                    lambda s: s.activation(Atm[:, :], pA, AF.Copy,
                                           scale=-(CCONST * LR),
                                           bias=-(CCONST * LR)),
                    waits=[("bte", te_A[t])], inc="bac")

                def a0(v, q=p):
                    v.tensor_tensor(Xb[:, q * 256:q * 256 + 128], Atm[:, :],
                                    mkL[:, :], op=ALU.mult)
                    v.tensor_tensor(Yb[:, q * 256:q * 256 + 128], Atm[:, :],
                                    mkU[:, :], op=ALU.mult)
                    v.drain()
                    return v.tensor_tensor(Pb[:, q * 128:(q + 1) * 128],
                                           Yb[:, q * 256:q * 256 + 128],
                                           idt[:, :], op=ALU.add)
                dv_A0[t] = p_.put("vector", a0, waits=[("bac", ac_atm[t])], inc="bdv")

                chain_sq(u, 4)
                chain_p(u, 4)

                def bmm(te):
                    last = None
                    for kk in range(DCH):
                        last = te.matmul(pB, KTb[:, kk * 128:(kk + 1) * 128],
                                         QTb[:, kk * 128:(kk + 1) * 128],
                                         start=(kk == 0), stop=(kk == DCH - 1))
                    return last
                te_B[t] = p_.put("tensor", bmm,
                                 waits=[("bac", ac_atm[t])], inc="bte")

                def bhf(v):
                    v.tensor_scalar_add(Btm[:, :], pB, 1.0)
                    v.drain()
                    return v.tensor_tensor(BhS[:, :], Btm[:, :], mkB[:, :],
                                           op=ALU.mult)
                dv_Bh[t] = p_.put("vector", bhf,
                                  waits=[("bte", te_B[t]),
                                         ("bsb", st_bh[t - 1] if t >= 1 else None)],
                                  inc="bdv")
                st_bh[t] = p_.put(
                    "gpsimd",
                    lambda g, t=t: g.dma_start(out=T["bh_out"](t), in_=BhS[:, :]),
                    waits=[("bdv", dv_Bh[t])], inc="bsb", inc_n=16)

                chain_sq(u, 5)
                chain_p(u, 5)
                chain_sq(u, 6)
                chain_p(u, 6)
                chain_mh(u)

                fA(t + 2)
                fB(t + 2)
            else:
                for k in range(1, 7):
                    chain_sq(u, k)
                    chain_p(u, k)
                chain_mh(u)

        if plan is None:
            realize(nc, p_, sems)
    L = n_tiles - 1
    return dict(ax0=p_.ctr.get("ax0", 0), ax1=p_.ctr.get("ax1", 0),
                bte=p_.ctr.get("bte", 0), adv=p_.ctr.get("adv", 0),
                aac=p_.ctr.get("aac", 0), bdv=p_.ctr.get("bdv", 0),
                bac=p_.ctr.get("bac", 0),
                epi=dv_epi[L], ylo=dv_ylo[L], evy=ac_evy[L],
                kev=ac_Kev[L], atm=ac_atm[L], lv=ac_lv[L][6],
                pa=dv_Pa[L][6], bh=dv_Bh[L], tek=te_K[L],
                qev=ac_Qev[L], vev=dv_Vev[L], vev_sem="bac")


A_SEMS = ("al0", "al1", "ax0", "ax1", "ate", "adv", "aac")
B_SEMS = ("bw0", "bw1", "bw2", "bw3", "bw4", "bw5", "bw6", "bw7", "bwq",
          "bwm", "bsv", "bsk", "bsq", "bsm", "bsb", "bte", "bdv", "bac")


def emit_phase2(nc, sems, n_chunks, T, plan=None):
    p_ = plan if plan is not None else Plan()
    with ExitStack() as _es:
        KTc = _es.enter_context(nc.sbuf_tensor("KTc", [128, 4 * D], F32))
        QTc = _es.enter_context(nc.sbuf_tensor("QTc", [128, 4 * D], F32))
        Ktm = _es.enter_context(nc.sbuf_tensor("Ktm", [128, D], F32))
        Vc = _es.enter_context(nc.sbuf_tensor("Vc", [128, 4 * 128], F32))
        Mh2 = _es.enter_context(nc.sbuf_tensor("Mh2", [128, 4 * 128], F32))
        Bh2 = _es.enter_context(nc.sbuf_tensor("Bh2", [128, 4 * 128], F32))
        St = _es.enter_context(nc.sbuf_tensor("St", [128, DCH * 128], F32))
        btS = _es.enter_context(nc.sbuf_tensor("btS", [1, 128], F32))
        Usb = _es.enter_context(nc.sbuf_tensor("Usb", [128, 128], F32))
        Gsb = _es.enter_context(nc.sbuf_tensor("Gsb", [128, 128], F32))
        Osb = _es.enter_context(nc.sbuf_tensor("Osb", [128, 8 * 128], F32))
        one1C = _es.enter_context(nc.sbuf_tensor("one1C", [1, 128], F32))
        oneC1 = _es.enter_context(nc.sbuf_tensor("oneC1", [128, 1], F32))
        idt3 = _es.enter_context(nc.sbuf_tensor("idt3", [128, 128], F32))
        ptrA = _es.enter_context(nc.psum_tensor("ptrA", [128, 512], F32))
        ptrB = _es.enter_context(nc.psum_tensor("ptrB", [128, 512], F32))
        pU = _es.enter_context(nc.psum_tensor("pU", [128, 128], F32))
        pG = _es.enter_context(nc.psum_tensor("pG", [128, 128], F32))
        pO = _es.enter_context(nc.psum_tensor("pO", [128, 128], F32))
        pbp = _es.enter_context(nc.psum_tensor("pbp", [1, 128], F32))
        pupA = _es.enter_context(nc.psum_tensor("pupA", [128, 512], F32))
        pupB = _es.enter_context(nc.psum_tensor("pupB", [128, 512], F32))
        ptr = [ptrA, ptrB]
        pup = [pupA, pupB]

        entry_w = T.get("entry_waits", [])
        p_.put("sync", lambda s: s.dma_start(
            out=St[:, :], in_=bass.AP(T["w0t"], 0, [[128, 128], [128 * 128, DCH], [1, 128]])),
            waits=entry_w, inc="cli", inc_n=16)
        p_.put("sync", lambda s: s.dma_start(out=btS[:, :], in_=T["b0s"].ap()),
               inc="cli", inc_n=16)
        p_.put("sync", lambda s: s.dma_start(out=idt3[:, :], in_=T["ident"].ap()),
               inc="cli", inc_n=16)
        dv_ones = p_.put(
            "vector",
            lambda v: (v.drain(), v.memset(one1C[:, :], 1.0),
                       v.memset(oneC1[:, :], 1.0))[-1],
            waits=entry_w, inc="cdv")
        CLI = 16 * 3
        for fn_, ws_, inc_, incn_ in T.get("gps_pre", []):
            p_.put("gpsimd", fn_, waits=ws_, inc=inc_, inc_n=incn_)
        for fn_, ws_, inc_, incn_ in T.get("sync_pre", []):
            p_.put("sync", fn_, waits=ws_, inc=inc_, inc_n=incn_)
        for fn_, ws_, inc_, incn_ in T.get("ac_pre", []):
            p_.put("scalar", fn_, waits=ws_, inc=inc_, inc_n=incn_)

        ld_c = [80 * (j // 4 + 1) for j in range(n_chunks)]
        te_tr = [0] * n_chunks
        te_U = [0] * n_chunks; te_G = [0] * n_chunks
        te_O = [0] * n_chunks; te_upd = [0] * n_chunks
        for j in range(n_chunks):
            te_tr[j] = 1 if j == 0 else 5 * j + 1
            te_U[j] = 5 * j + 2
            te_G[j] = 5 * j + 3
            te_O[j] = 5 * j + 4
            te_upd[j] = 5 * j + 5
        dv_Uev = [3 * j + 2 for j in range(n_chunks)]
        dv_Oev = [3 * j + 3 for j in range(n_chunks)]
        dv_St = [3 * j + 4 for j in range(n_chunks)]
        ac_Ktm = [2 * j + 1 for j in range(n_chunks)]
        ac_G = [2 * j + 2 for j in range(n_chunks)]
        st_O = [16 * (j // 8 + 1) for j in range(n_chunks)]

        extra_w = T.get("ld_waits", lambda j: [])

        for j in range(n_chunks):
            p = j % 4
            CL = "cl" + str(p)
            SO = "so%d" % (j % 8)
            KTa = KTc[:, p * D:(p + 1) * D]
            QTa = QTc[:, p * D:(p + 1) * D]
            Va = Vc[:, p * 128:(p + 1) * 128]
            Mha = Mh2[:, p * 128:(p + 1) * 128]
            Bha = Bh2[:, p * 128:(p + 1) * 128]
            Oa = Osb[:, (j % 8) * 128:(j % 8 + 1) * 128]

            for fn_, ws_, inc_, incn_ in T.get("sync_pre_at", {}).get(j, []):
                p_.put("sync", fn_, waits=ws_, inc=inc_, inc_n=incn_)
            for fn_, ws_, inc_, incn_ in T.get("gps_pre_at", {}).get(j, []):
                p_.put("gpsimd", fn_, waits=ws_, inc=inc_, inc_n=incn_)

            def lds(s, KTa=KTa, QTa=QTa, Va=Va, Mha=Mha, Bha=Bha, j=j, CL=CL):
                s.dma_start(out=KTa, in_=T["kt_in"](j)).then_inc(sems[CL], 16)
                s.dma_start(out=QTa, in_=T["qt_in"](j)).then_inc(sems[CL], 16)
                s.dma_start(out=Va, in_=T["v_in"](j)).then_inc(sems[CL], 16)
                s.dma_start(out=Mha, in_=T["mh_in"](j)).then_inc(sems[CL], 16)
                return s.dma_start(out=Bha, in_=T["bh_in"](j))
            p_.ctr[CL] = p_.ctr.get(CL, 0) + 64
            assert ld_c[j] == p_.put(
                "sync", lds,
                waits=[("cte", te_upd[j - 4] if j >= 4 else None),
                       ("cdv", dv_Uev[j - 4] if j >= 4 else None)] + extra_w(j),
                inc=CL, inc_n=16)

            def trs(te, KTa=KTa):
                last = None
                for kk in range(DCH):
                    last = te.transpose(ptr[kk // 4][:, (kk % 4) * 128:(kk % 4 + 1) * 128],
                                        KTa[:, kk * 128:(kk + 1) * 128], idt3[:, :])
                return last

            if j == 0:
                assert te_tr[0] == p_.put(
                    "tensor", trs,
                    waits=[("cl0", ld_c[0]), ("cli", CLI)]
                    + T.get("w_trs0", []),
                    inc="cte")

            def umm(te, KTa=KTa):
                for kk in range(DCH):
                    te.matmul(pU[:, :], KTa[:, kk * 128:(kk + 1) * 128],
                              St[:, kk * 128:(kk + 1) * 128],
                              start=(kk == 0), stop=False)
                return te.matmul(pU[:, :], one1C[:, :], btS[:, :],
                                 start=False, stop=True)
            assert te_U[j] == p_.put(
                "tensor", umm,
                waits=[("cdv", dv_St[j - 1] if j >= 1 else dv_ones),
                       (CL, ld_c[j])]
                + (T.get("w_u0", []) if j == 0 else []),
                inc="cte")

            def qmm(te, QTa=QTa, half=0):
                last = None
                for kk in range(half * 4, half * 4 + 4):
                    last = te.matmul(pO[:, :], QTa[:, kk * 128:(kk + 1) * 128],
                                     St[:, kk * 128:(kk + 1) * 128],
                                     start=(kk == 0), stop=False)
                if half == 0:
                    return last
                return te.matmul(pO[:, :], one1C[:, :], btS[:, :],
                                 start=False, stop=False)
            p_.put("tensor", lambda te, QTa=QTa: qmm(te, QTa, 0),
                   waits=(T.get("w_qs0", []) if j == 0 else []))

            assert te_G[j] == p_.put(
                "tensor",
                lambda te, Mha=Mha: te.matmul(pG[:, :], Mha, Usb[:, :],
                                              start=True, stop=True),
                waits=[("cdv", dv_Uev[j]), ("cac", ac_Ktm[j])]
                + (T.get("w_g0", []) if j == 0 else []),
                inc="cte")

            p_.put("tensor", lambda te, QTa=QTa: qmm(te, QTa, 1))

            assert te_O[j] == p_.put(
                "tensor",
                lambda te, Bha=Bha: te.matmul(pO[:, :], Bha, Gsb[:, :],
                                              start=False, stop=True),
                waits=[("cac", ac_G[j])],
                inc="cte")

            def updmm(te):
                for kk in range(DCH):
                    te.matmul(pup[kk // 4][:, (kk % 4) * 128:(kk % 4 + 1) * 128],
                              Ktm[:, kk * 128:(kk + 1) * 128], Gsb[:, :],
                              start=True, stop=True)
                return te.matmul(pbp[:, :], oneC1[:, :], Gsb[:, :],
                                 start=True, stop=True)
            assert te_upd[j] == p_.put(
                "tensor", updmm,
                waits=[("cac", ac_G[j])]
                + (T.get("w_upd0", []) if j == 0 else []),
                inc="cte")

            if j + 1 < n_chunks:
                pn = (j + 1) % 4
                KTn = KTc[:, pn * D:(pn + 1) * D]
                assert te_tr[j + 1] == p_.put(
                    "tensor", lambda te, KTn=KTn: trs(te, KTn),
                    waits=[("cl" + str(pn), ld_c[j + 1]),
                           ("cac", ac_Ktm[j])],
                    inc="cte")

            assert dv_Uev[j] == p_.put(
                "vector",
                lambda v, Va=Va: v.tensor_tensor(Usb[:, :], pU[:, :], Va, op=ALU.subtract),
                waits=[("cte", te_U[j]), (CL, ld_c[j])],
                inc="cdv")
            assert dv_Oev[j] == p_.put(
                "vector",
                lambda v, Oa=Oa: v.tensor_copy(Oa, pO[:, :]),
                waits=[("cte", te_O[j]),
                       (SO, st_O[j - 8] if j >= 8 else None)],
                inc="cdv")

            def stupd(v):
                v.tensor_tensor(St[:, 0:512], St[:, 0:512], pupA[:, :], op=ALU.add)
                v.tensor_tensor(St[:, 512:1024], St[:, 512:1024], pupB[:, :], op=ALU.add)
                return v.tensor_tensor(btS[:, :], btS[:, :], pbp[:, :], op=ALU.add)
            assert dv_St[j] == p_.put("vector", stupd, waits=[("cte", te_upd[j])],
                                      inc="cdv")

            def ktmev(s):
                s.copy(Ktm[:, 0:512], ptrA[:, :])
                return s.copy(Ktm[:, 512:1024], ptrB[:, :])
            assert ac_Ktm[j] == p_.put("scalar", ktmev,
                                       waits=[("cte", te_tr[j])], inc="cac")
            assert ac_G[j] == p_.put("scalar", lambda s: s.copy(Gsb[:, :], pG[:, :]),
                                     waits=[("cte", te_G[j])], inc="cac")

            assert st_O[j] == p_.put(
                "gpsimd",
                lambda g, Oa=Oa, j=j: g.dma_start(out=T["o_out"](j), in_=Oa),
                waits=[("cdv", dv_Oev[j])], inc=SO, inc_n=16)
            hk = T.get("chunk_done_hook")
            if hk:
                hk(p_, j, st_O)

        if plan is None:
            realize(nc, p_, sems)
    return dict(cte=p_.ctr.get("cte", 0), cdv=p_.ctr.get("cdv", 0),
                cac=p_.ctr.get("cac", 0),
                so=[("so%d" % k, p_.ctr.get("so%d" % k, 0)) for k in range(8)])


C_SEMS = tuple(["cl0", "cl1", "cl2", "cl3", "cli", "cte", "cdv", "cac"]
               + ["so%d" % k for k in range(8)])


def emit_phase3(nc, sems, n_tiles, T, plan=None):
    p_ = plan if plan is not None else Plan()
    nh = 2 if n_tiles >= 2 else 1
    H = n_tiles // nh
    with ExitStack() as _es:
        sw16 = T["sw16g"]
        x1t = _es.enter_context(nc.sbuf_tensor("x1t", [128, 4 * D], F32))
        ot = _es.enter_context(nc.sbuf_tensor("ot", [128, 8 * D], F32))
        x2b = _es.enter_context(nc.sbuf_tensor("x2b", [128, 4 * D], F32))
        x3b = _es.enter_context(nc.sbuf_tensor("x3b", [128, 2 * D], F32))
        xn3 = _es.enter_context(nc.sbuf_tensor("xn3", [128, 2 * D], F32))
        xn3Th = _es.enter_context(nc.sbuf_tensor("xn3Th", [128, 2 * D], BF16))
        xn3Tl = _es.enter_context(nc.sbuf_tensor("xn3Tl", [128, 2 * D], BF16))
        scr = _es.enter_context(nc.sbuf_tensor("scr3", [128, D], F32))
        scrD = _es.enter_context(nc.sbuf_tensor("scrD3", [128, D], F32))
        idt = _es.enter_context(nc.sbuf_tensor("idt4", [128, 128], F32))
        stat = _es.enter_context(nc.sbuf_tensor("stat3", [128, 16], F32))
        qf00 = _es.enter_context(nc.psum_tensor("qf00", [128, 512], F32))
        qf01 = _es.enter_context(nc.psum_tensor("qf01", [128, 512], F32))
        qf10 = _es.enter_context(nc.psum_tensor("qf10", [128, 512], F32))
        qf11 = _es.enter_context(nc.psum_tensor("qf11", [128, 512], F32))
        rt00 = _es.enter_context(nc.psum_tensor("rt00", [128, 512], F32))
        rt01 = _es.enter_context(nc.psum_tensor("rt01", [128, 512], F32))
        rt10 = _es.enter_context(nc.psum_tensor("rt10", [128, 512], F32))
        rt11 = _es.enter_context(nc.psum_tensor("rt11", [128, 512], F32))
        qfp = [[qf00, qf01], [qf10, qf11]]
        ptrp = [[rt00, rt01], [rt10, rt11]]

        entry_w = T.get("entry_waits", [])
        p_.put("sync", lambda s: s.dma_start(out=idt[:, :], in_=T["ident"].ap()),
               waits=entry_w, inc="dwl", inc_n=16)
        for fn_, ws_, inc_, incn_ in T.get("gps_pre", []):
            p_.put("gpsimd", fn_, waits=ws_, inc=inc_, inc_n=incn_)

        te_tr = [3 * t + 1 for t in range(n_tiles)]
        te_fh = [3 * t + 2 for t in range(n_tiles)]
        te_ffn = [3 * t + 3 for t in range(n_tiles)]
        ld_x = [32 * (t // 4 + 1) for t in range(n_tiles)]
        st_y = [16 * (t // 2 + 1) for t in range(n_tiles)]
        dv_x2 = [0] * n_tiles; dv_xn3 = [0] * n_tiles; dv_out = [0] * n_tiles
        dv_lo = [0] * n_tiles
        ac_st = [0] * n_tiles; ac_ev = [0] * n_tiles

        def slots(t):
            p = t % 2
            q4 = t % 4
            return dict(
                p=p,
                x1a=x1t[:, q4 * D:(q4 + 1) * D],
                oa=ot[:, (t % 8) * D:(t % 8 + 1) * D],
                x2a=x2b[:, q4 * D:(q4 + 1) * D],
                x3a=x3b[:, p * D:(p + 1) * D],
                xna=xn3[:, p * D:(p + 1) * D],
                xnha=xn3Th[:, p * D:(p + 1) * D],
                xnla=xn3Tl[:, p * D:(p + 1) * D],
                sx=stat[:, p * 8 + 0:p * 8 + 1],
                sq=stat[:, p * 8 + 1:p * 8 + 2],
                mu=stat[:, p * 8 + 2:p * 8 + 3],
                inv=stat[:, p * 8 + 3:p * 8 + 4],
                e2=stat[:, p * 8 + 4:p * 8 + 5],
                m2=stat[:, p * 8 + 5:p * 8 + 6])

        def front1(t):
            if t >= n_tiles:
                return
            S_ = slots(t)
            DL = "dl" + str(t % 4)

            xw = T.get("x1_waits", []) if t == 0 else []
            p_.put("sync",
                   lambda s, S_=S_, t=t: s.dma_start(
                       out=S_["x1a"], in_=T["x1s"][t * C:(t + 1) * C, :]),
                   waits=[("ddv", dv_x2[t - 4] if t >= 4 else None),
                          (DL, 32 * (t // 4) if t >= 4 else None)] + xw,
                   inc=DL, inc_n=16)

            if t % 4 == 0:
                batch = [tt for tt in range(t, t + 4) if tt < n_tiles]

                def old_(s, t=t, batch=batch):
                    rk = s.cc_rank(replica_groups=T["rg"])
                    for i in s.Switch(rk, NCORES):
                        for tt in batch:
                            if tt % H == 0:
                                T["o_half_wait"](s, i, tt // H)
                            s.dma_start(
                                out=slots(tt)["oa"],
                                in_=T["o_in_direct"](i, tt)).then_inc(
                                sems["dl" + str(tt % 4)], 16)
                    return None
                for tt in batch:
                    p_.ctr["dl" + str(tt % 4)] = (
                        p_.ctr.get("dl" + str(tt % 4), 0) + 16)
                p_.put("sync", old_,
                       waits=[("ddv", dv_x2[t - 5] if t >= 5 else None)]
                       + [("dl" + str(tt % 4),
                           32 * (tt // 4) if tt >= 4 else None)
                          for tt in batch])

            def mkx2(v, S_=S_, t=t):
                if t == 0:
                    v.drain()
                v.tensor_tensor(S_["x2a"], S_["x1a"], S_["oa"], op=ALU.add)
                v.drain()
                return v.tensor_scalar(scrD[:, :], S_["x2a"], 1.0, None,
                                       op0=ALU.mult, op1=ALU.add,
                                       accum_out=S_["sx"])
            dv_x2[t] = p_.put("vector", mkx2,
                              waits=[(DL, ld_x[t]),
                                     ("dac", ac_st[t - 2] if t >= 2 else None),
                                     ("ddv", dv_out[t - 4] if t >= 4 else None)],
                              inc="ddv")

        def front2a(t):
            if t >= n_tiles:
                return
            S_ = slots(t)
            def sqf(v, S_=S_):
                v.drain()
                return v.scalar_tensor_tensor(
                    scrD[:, :], S_["x2a"], 1.0, S_["x2a"],
                    op0=ALU.mult, op1=ALU.mult, accum_out=S_["sq"])
            p_.put("vector", sqf, inc="ddv")

            def mini(v, S_=S_):
                v.drain()
                v.tensor_tensor(S_["m2"], S_["sx"], S_["sx"], op=ALU.mult)
                v.tensor_scalar(S_["e2"], S_["sq"], 1.0 / D, EPS,
                                op0=ALU.mult, op1=ALU.add)
                v.tensor_scalar(S_["mu"], S_["sx"], 1.0 / D, None, op0=ALU.mult)
                v.drain()
                return v.scalar_tensor_tensor(S_["m2"], S_["m2"], -1.0 / (D * D),
                                              S_["e2"], op0=ALU.mult, op1=ALU.add)
            dv_m = p_.put("vector", mini, inc="ddv")

            def sqr(s, S_=S_):
                s.drain()
                return s.activation(S_["sq"], S_["m2"], AF.Sqrt)
            ac_st[t] = p_.put("scalar", sqr, waits=[("ddv", dv_m)], inc="dac")

        def front2b(t):
            if t >= n_tiles:
                return
            S_ = slots(t)

            def xn3f(v, S_=S_):
                v.reciprocal(S_["inv"], S_["sq"])
                v.drain()
                return v.tensor_scalar(S_["xna"], S_["x2a"], S_["mu"], S_["inv"],
                                       op0=ALU.subtract, op1=ALU.mult)
            dv_xn3[t] = p_.put(
                "vector", xn3f,
                waits=[("dac", ac_st[t]),
                       ("dte", te_tr[t - 2] if t >= 2 else None)],
                inc="ddv")

        def mid(t):
            S_ = slots(t)
            p = S_["p"]
            rtr = ptrp[p]

            def trs(te, S_=S_, rtr=rtr):
                last = None
                for kk in range(DCH):
                    last = te.transpose(
                        rtr[kk // 4][:, (kk % 4) * 128:(kk % 4 + 1) * 128],
                        S_["xna"][:, kk * 128:(kk + 1) * 128], idt[:, :])
                return last
            assert te_tr[t] == p_.put(
                "tensor", trs,
                waits=[("ddv", dv_xn3[t]), ("dwl", 16),
                       ("dac", ac_ev[t - 2] if t >= 2 else None),
                       ("ddv", dv_lo[t - 2] if t >= 2 else None)]
                + (T.get("entry_te_waits", []) if t == 0 else []),
                inc="dte")

            def evacs(s, S_=S_, rtr=rtr):
                s.copy(S_["xnha"][:, 0:512], rtr[0][:, :])
                return s.copy(S_["xnha"][:, 512:1024], rtr[1][:, :])
            ac_ev[t] = p_.put("scalar", evacs, waits=[("dte", te_tr[t])], inc="dac")

            def evlo(v, S_=S_, rtr=rtr):
                v.tensor_tensor(S_["xnla"][:, 0:512], rtr[0][:, :],
                                S_["xnha"][:, 0:512], op=ALU.subtract)
                return v.tensor_tensor(S_["xnla"][:, 512:1024], rtr[1][:, :],
                                       S_["xnha"][:, 512:1024], op=ALU.subtract)
            dv_lo[t] = p_.put("vector", evlo,
                              waits=[("dac", ac_ev[t]), ("dte", te_tr[t])],
                              inc="ddv")

            def ffn_hi(te, S_=S_, p=p):
                last = None
                for h in (0, 1):
                    for kk in range(DCH):
                        last = te.matmul(
                            qfp[p][h][:, :], S_["xnha"][:, kk * 128:(kk + 1) * 128],
                            sw16[:, kk * D + h * 512: kk * D + h * 512 + 512],
                            start=(kk == 0), stop=False)
                return last
            assert te_fh[t] == p_.put(
                "tensor", ffn_hi,
                waits=[("dac", ac_ev[t]),
                       ("ddv", dv_out[t - 2] if t >= 2 else None)],
                inc="dte")

            def ffn_lo(te, S_=S_, p=p):
                last = None
                for h in (0, 1):
                    for kk in range(DCH):
                        last = te.matmul(
                            qfp[p][h][:, :], S_["xnla"][:, kk * 128:(kk + 1) * 128],
                            sw16[:, kk * D + h * 512: kk * D + h * 512 + 512],
                            start=False, stop=(kk == DCH - 1))
                return last
            assert te_ffn[t] == p_.put(
                "tensor", ffn_lo, waits=[("ddv", dv_lo[t])], inc="dte")

        def back(t):
            S_ = slots(t)
            p = S_["p"]

            def outw(v, S_=S_, p=p):
                v.scalar_tensor_tensor(S_["x3a"][:, 0:512], qfp[p][0][:, :], 0.5,
                                       S_["x2a"][:, 0:512], op0=ALU.mult, op1=ALU.add)
                return v.scalar_tensor_tensor(S_["x3a"][:, 512:1024], qfp[p][1][:, :],
                                              0.5, S_["x2a"][:, 512:1024],
                                              op0=ALU.mult, op1=ALU.add)
            dv_out[t] = p_.put("vector", outw,
                               waits=[("dte", te_ffn[t]),
                                      ("sy" + str(p), st_y[t - 2] if t >= 2 else None)],
                               inc="ddv")
            assert st_y[t] == p_.put(
                "gpsimd",
                lambda g, S_=S_, t=t: g.dma_start(
                    out=T["y"][t * C:(t + 1) * C, :], in_=S_["x3a"]),
                waits=[("ddv", dv_out[t])], inc="sy" + str(p), inc_n=16)

        front1(0); front2a(0); front2b(0)
        front1(1)
        for t in range(n_tiles):
            mid(t)
            front2a(t + 1)
            front2b(t + 1)
            back(t)
            front1(t + 2)

        p_.put("gpsimd", lambda g: g.nop(),
               waits=[("sy0", p_.ctr.get("sy0", 0)), ("sy1", p_.ctr.get("sy1", 0))])
        if plan is None:
            realize(nc, p_, sems)


D_SEMS = ("dl0", "dl1", "dl2", "dl3", "dwl", "sy0", "sy1", "dte", "ddv", "dac")


GK_TILE = 2 * DCH * 128 * 128 + 2 * 128 * 128
NG = 8


def build_fused(n_tiles=NT):
    nc = bass.Bass("TRN2", target_bir_lowering=False, debug=False)
    S = n_tiles * C
    n_chunks = NCORES * n_tiles
    ng = min(NG, n_tiles)
    if n_tiles == 16:
        GSZ = [2] * 8
    else:
        GSZ = [n_tiles // ng] * ng
        GSZ[-1] += n_tiles - sum(GSZ)
    GOF = [sum(GSZ[:g]) for g in range(ng)]
    G_OF_T = []
    for g in range(ng):
        G_OF_T += [g] * GSZ[g]

    dr = {}
    dr["x"] = nc.dram_tensor("x", [S, D], F32, kind="ExternalInput")
    for nm in ("fwT", "thKT", "thVT", "thQT"):
        dr[nm] = nc.dram_tensor(nm, [D, D], F32, kind="ExternalInput")
    for nm in ("ident", "maskL", "maskU", "maskB"):
        dr[nm] = nc.dram_tensor(nm, [128, 128], F32, kind="ExternalInput")
    dr["w0t"] = nc.dram_tensor("w0t", [D, RS], F32, kind="ExternalInput")
    dr["b0s"] = nc.dram_tensor("b0s", [1, RS], F32, kind="ExternalInput")
    dr["y"] = nc.dram_tensor("y", [S, D], F32, kind="ExternalOutput")

    dr["x1"] = nc.dram_tensor("x1loc", [S, D], F32)
    dr["x1s"] = dr["x1"]
    gk = nc.dram_tensor("gk", [n_tiles, GK_TILE], F32)
    gk_all = [nc.dram_tensor("gk_all%d" % g, [NCORES * GSZ[g], GK_TILE], F32,
                             addr_space="Shared")
              for g in range(ng)]
    va_in = [nc.dram_tensor("va_in%d" % g, [NCORES, GSZ[g] * 128, RS], F32)
             for g in range(ng)]
    va_all = [nc.dram_tensor("va_all%d" % g, [NCORES * NCORES, GSZ[g] * 128, RS],
                             F32, addr_space="Shared")
              for g in range(ng)]
    va_my = nc.dram_tensor("va_my", [NCORES, S, RS], F32)
    oa_in = nc.dram_tensor("oa_in", [NCORES, S, RS], F32)
    nh = 2 if n_tiles >= 2 else 1
    H = n_tiles // nh
    oa_half = [nc.dram_tensor("oa_half%d" % q, [NCORES, H * 128, RS], F32,
                              addr_space="Shared")
               for q in range(NCORES * nh)]

    KQ = DCH * 128 * 128
    dr["kt_out"] = lambda t: bass.AP(gk, t * GK_TILE,
                                     [[128, 128], [128 * 128, DCH], [1, 128]])
    dr["qt_out"] = lambda t: bass.AP(gk, t * GK_TILE + KQ,
                                     [[128, 128], [128 * 128, DCH], [1, 128]])
    dr["mh_out"] = lambda t: bass.AP(gk, t * GK_TILE + 2 * KQ, [[128, 128], [1, 128]])
    dr["bh_out"] = lambda t: bass.AP(gk, t * GK_TILE + 2 * KQ + 128 * 128,
                                     [[128, 128], [1, 128]])
    def _vout(t):
        g = G_OF_T[t]
        return bass.AP(va_in[g], (t - GOF[g]) * 128 * RS,
                       [[RS, 128], [GSZ[g] * 128 * RS, NCORES], [1, RS]])
    dr["v_out"] = _vout

    def _gkrow(j):
        tile = j % n_tiles
        g = G_OF_T[tile]
        loc = tile - GOF[g]
        return g, ((j // n_tiles) * GSZ[g] + loc) * GK_TILE
    dr["kt_in"] = lambda j: bass.AP(gk_all[_gkrow(j)[0]], _gkrow(j)[1],
                                    [[128, 128], [128 * 128, DCH], [1, 128]])
    dr["qt_in"] = lambda j: bass.AP(gk_all[_gkrow(j)[0]], _gkrow(j)[1] + KQ,
                                    [[128, 128], [128 * 128, DCH], [1, 128]])
    dr["mh_in"] = lambda j: bass.AP(gk_all[_gkrow(j)[0]], _gkrow(j)[1] + 2 * KQ,
                                    [[128, 128], [1, 128]])
    dr["bh_in"] = lambda j: bass.AP(gk_all[_gkrow(j)[0]],
                                    _gkrow(j)[1] + 2 * KQ + 128 * 128,
                                    [[128, 128], [1, 128]])
    dr["v_in"] = lambda j: bass.AP(va_my, (j // n_tiles) * S * RS
                                   + (j % n_tiles) * 128 * RS, [[RS, 128], [1, RS]])
    dr["o_out"] = lambda j: bass.AP(oa_in, (j // n_tiles) * S * RS
                                    + (j % n_tiles) * 128 * RS, [[RS, 128], [1, RS]])
    dr["o_in_direct"] = lambda i, t: bass.AP(
        oa_half[i * nh + t // H], (t % H) * 128 * RS,
        [[RS, 128], [H * 128 * RS, NCORES], [1, RS]])

    rg = [list(range(NCORES))]
    sems_ref = [None]

    def tile_done_hook(p_, u, stor):
        if u + 1 not in [GOF[g] + GSZ[g] for g in range(ng)]:
            return
        g = G_OF_T[u]
        ws = [("bsv", stor["st_v"][u]), ("bsk", stor["st_kt"][u]),
              ("bsq", stor["st_qt"][u]), ("bsm", stor["st_mh"][u]),
              ("bsb", stor["st_bh"][u])]
        p_.put("gpsimd",
               lambda gps, g=g: gps.collective_compute(
                   "AllGather", ALU.bypass, replica_groups=rg,
                   ins=[gk[GOF[g]:GOF[g] + GSZ[g], :]], outs=[gk_all[g].ap()]),
               waits=ws, inc="cg%d" % g)
        p_.put("gpsimd",
               lambda gps, g=g: gps.collective_compute(
                   "AllGather", ALU.bypass, replica_groups=rg,
                   ins=[va_in[g].ap()], outs=[va_all[g].ap()]),
               inc="cv%d" % g)
    dr["tile_done_hook"] = tile_done_hook

    def mk_vex(g):
        def vex(s):
            if g >= 1:
                s.wait_ge(sems_ref[0]["vex"], 16 * NCORES * g)
            rk = s.cc_rank(replica_groups=rg)
            for i in s.Switch(rk, NCORES):
                for src in range(NCORES):
                    s.dma_start(
                        out=va_my[src, GOF[g] * 128:(GOF[g] + GSZ[g]) * 128, :],
                        in_=va_all[g][src * NCORES + i, :, :]).then_inc(
                        sems_ref[0]["vex"], 16)
            return None
        return vex

    dr2 = dict(dr)
    gps_at = {}
    gpa = {}
    spa = {}
    pos_map = ({0: 4, 1: 6, 2: 8, 3: 10, 4: 12, 5: 14}
               if n_tiles == 16 else {})
    for g in range(ng):
        item = (mk_vex(g), [("cv%d" % g, 1)], None, 1)
        if g in pos_map:
            gps_at.setdefault(pos_map[g], []).append(item)
        elif n_tiles == 16 and g == 6:
            gpa.setdefault(4, []).append(item)
        elif n_tiles == 16 and g == 7:
            gpa.setdefault(12, []).append(item)
        else:
            gpa.setdefault(max(GOF[g] - 4, 0), []).append(item)
    dr2["gps_at"] = gps_at
    dr2["gps_pre_at"] = gpa
    dr2["sync_pre_at"] = spa
    dr2["ld_waits"] = lambda j: [
        ("cg%d" % G_OF_T[j % n_tiles], 1),
        ("vex", 16 * NCORES * (G_OF_T[j % n_tiles] + 1))]

    def chunk_done_hook(p_, j, st_O):
        if (j + 1) % H != 0:
            return
        d = j // n_tiles
        half = (j % n_tiles) // H
        ws = [("so%d" % k, p_.ctr.get("so%d" % k, 0)) for k in range(8)]
        p_.put("gpsimd",
               lambda gps, d=d, half=half: gps.collective_compute(
                   "AllGather", ALU.bypass, replica_groups=rg,
                   ins=[oa_in[d:d + 1, half * H * 128:(half + 1) * H * 128, :]],
                   outs=[oa_half[d * nh + half].ap()]),
               waits=ws, inc="coh%d" % (d * nh + half))
    dr2["chunk_done_hook"] = chunk_done_hook

    dr3 = dict(dr)
    dr3["rg"] = rg
    dr3["o_half_wait"] = lambda s, i, hf: s.wait_ge(
        sems_ref[0]["coh%d" % (i * nh + hf)], 1)

    with ExitStack() as _es:
        extra = tuple(["cg%d" % g for g in range(ng)]
                      + ["cv%d" % g for g in range(ng)]
                      + ["coh%d" % q for q in range(NCORES * nh)] + ["vex"])
        sems = mk_sems(nc, _es, A_SEMS + B_SEMS + C_SEMS + D_SEMS + extra)
        sems_ref[0] = sems
        sw16g = _es.enter_context(nc.sbuf_tensor("sw16g", [128, DCH * D], BF16))
        dr2["sw16g"] = sw16g
        dr3["sw16g"] = sw16g
        plan = Plan()
        prev = emit_pass_ab(nc, sems, n_tiles, dr2, plan=plan)
        dr3["x1_waits"] = [("ax0", prev["ax0"]), ("ax1", prev["ax1"])]
        dr2["entry_waits"] = [("bte", prev["tek"])]
        dr2["w_trs0"] = [("adv", prev["epi"]),
                         ("bac", max(prev["qev"], prev["vev"]))]
        dr2["w_u0"] = [("bac", prev["kev"])]
        dr2["w_qs0"] = [("aac", prev["evy"]), ("adv", prev["ylo"])]
        dr2["w_g0"] = [("bac", prev["kev"])]
        dr2["w_upd0"] = [("bac", prev["lv"]), ("bdv", prev["pa"]),
                         ("aac", prev["evy"]), ("adv", prev["ylo"])]
        p2 = emit_phase2(nc, sems, n_chunks, dr2, plan=plan)
        dr3["entry_waits"] = [("cte", p2["cte"])] + p2["so"]
        dr3["entry_te_waits"] = [("cdv", p2["cdv"]), ("cac", p2["cac"])]
        emit_phase3(nc, sems, n_tiles, dr3, plan=plan)
        realize(nc, plan, sems)
    return nc


def kernel(L0_attn_in_seq, L0_attn_theta_K, L0_attn_theta_V, L0_attn_theta_Q,
           L0_attn_W0, L0_attn_b0, L0_ffn_weight):
    x = np.ascontiguousarray(np.asarray(L0_attn_in_seq, np.float32).reshape(S_TOT, D))
    thK = np.asarray(L0_attn_theta_K, np.float32)
    thV = np.asarray(L0_attn_theta_V, np.float32)
    thQ = np.asarray(L0_attn_theta_Q, np.float32)
    W0 = np.asarray(L0_attn_W0, np.float32)
    b0 = np.asarray(L0_attn_b0, np.float32)
    fw = np.asarray(L0_ffn_weight, np.float32)

    fwT = np.ascontiguousarray(fw.T)
    thKT = np.ascontiguousarray(thK.T)
    thVT = np.ascontiguousarray(thV.T)
    thQT = np.ascontiguousarray(thQ.T)
    ident, maskL, maskU, maskB = np_consts()

    nc = build_fused()
    in_maps = []
    for c in range(NCORES):
        in_maps.append(dict(
            x=np.ascontiguousarray(x[c * TPC:(c + 1) * TPC]),
            fwT=fwT, thKT=thKT, thVT=thVT, thQT=thQT,
            ident=ident, maskL=maskL, maskU=maskU, maskB=maskB,
            w0t=np.ascontiguousarray(W0[c * RS:(c + 1) * RS, :].T),
            b0s=np.ascontiguousarray(b0[c * RS:(c + 1) * RS]).reshape(1, RS)))
    res = run_bass_kernel_spmd(nc, in_maps, core_ids=list(range(NCORES))).results
    y = np.concatenate([r["y"] for r in res], 0)
    return y.reshape(1, S_TOT, D).astype(np.float32)
